# version 20
# baseline (speedup 1.0000x reference)
import contextlib

import numpy as np
import ml_dtypes

import concourse.bass as bass
import concourse.mybir as mybir
from concourse.bass_utils import run_bass_kernel_spmd

F32 = mybir.dt.float32
BF16 = mybir.dt.bfloat16
I32 = mybir.dt.int32
AF = mybir.ActivationFunctionType
ALU = mybir.AluOpType
BFNP = ml_dtypes.bfloat16

C_OFF = 30.0
EPS = 1e-5
DEQ = float(np.float32(1.0 / 127.0))


class Tok:

    def __init__(self, handle, step=1):
        self.h = handle
        self.step = step
        self.n = 0

    def inc(self, ins, expect=None):
        self.n += self.step
        ins.then_inc(self.h, self.step)
        if expect is not None:
            assert self.n == expect * self.step, (self.n, expect, self.step)

    def wait(self, eng, expect):
        v = int(expect) * self.step
        if v > 0:
            eng.wait_ge(self.h, v)


class SelfSem:

    def __init__(self, handle):
        self.h = handle
        self.n = 0

    def edge(self, eng, ins):
        self.n += 1
        ins.then_inc(self.h, 1)
        eng.wait_ge(self.h, self.n)


def build_program(SEQ=8192, DM=1024, HID=3072, debug=False, phases=3, asub=3, bsub=6):
    assert DM == 1024 and HID % 512 == 0 and SEQ % 2048 == 0
    OWN = SEQ // 2
    DC = DM // 128
    HC = HID // 128
    AG = 512
    NAG = SEQ // AG
    NOG = OWN // AG
    BQ = 1024
    NBG = OWN // BQ
    QT = BQ // 128
    SCn = 1024
    NSC = SEQ // SCn
    KCS = SCn // 128
    CG = 512
    NCG = OWN // CG
    CT = CG // 128
    HSUP = 4
    NHS = HC // HSUP
    NW = 4 * DC
    NI = 2 * DC + HC
    NU = 4 * DC + HC
    HH = HID // 2
    INV_DM = float(np.float32(1.0 / DM))

    nc = bass.Bass("TRN2", target_bir_lowering=False, debug=False)

    x_own = nc.dram_tensor("x_own", [OWN, DM], F32, kind="ExternalInput").ap()
    x_oth = nc.dram_tensor("x_oth", [OWN, DM], F32, kind="ExternalInput").ap()
    wk_t = nc.dram_tensor("wk_t", [DM, DM], F32, kind="ExternalInput").ap()
    wv_t = nc.dram_tensor("wv_t", [DM, DM], F32, kind="ExternalInput").ap()
    wq_t = nc.dram_tensor("wq_t", [DM, DM], F32, kind="ExternalInput").ap()
    wo_t = nc.dram_tensor("wo_t", [DM, DM], F32, kind="ExternalInput").ap()
    bq_c = nc.dram_tensor("bq_c", [128, DC], F32, kind="ExternalInput").ap()
    bk_c = nc.dram_tensor("bk_c", [128, DC], F32, kind="ExternalInput").ap()
    bo_r = nc.dram_tensor("bo_r", [1, DM], F32, kind="ExternalInput").ap()
    bv_c = nc.dram_tensor("bv_c", [128, DC], F32, kind="ExternalInput").ap()
    wg_i = nc.dram_tensor("wg_i", [DM, HID], I32, kind="ExternalInput").ap()
    wu_i = nc.dram_tensor("wu_i", [DM, HID], I32, kind="ExternalInput").ap()
    wd_i = nc.dram_tensor("wd_i", [HID, DM], I32, kind="ExternalInput").ap()
    ident_h = nc.dram_tensor("ident_h", [128, 128], BF16, kind="ExternalInput").ap()
    ones_h = nc.dram_tensor("ones_h", [128, 512], BF16, kind="ExternalInput").ap()
    consts_h = nc.dram_tensor("consts_h", [128, 4], F32, kind="ExternalInput").ap()

    y_out = nc.dram_tensor("y_out", [OWN, DM], F32, kind="ExternalOutput").ap()

    scr = "ExternalOutput" if debug else "Internal"
    kT_scr = nc.dram_tensor("kT_scr", [DC, 128, SEQ], BF16, kind=scr).ap()
    v_scr = nc.dram_tensor("v_scr", [SEQ // 128, 128, DM], BF16, kind=scr).ap()
    qT_scr = nc.dram_tensor("qT_scr", [DC, 128, OWN], BF16, kind=scr).ap()
    x2_scr = nc.dram_tensor("x2_scr", [OWN, DM], BF16, kind=scr).ap()
    wg_scr = nc.dram_tensor("wg_scr", [DC, 128, HID], BF16, kind=scr).ap()
    wu_scr = nc.dram_tensor("wu_scr", [DC, 128, HID], BF16, kind=scr).ap()
    wd_scr = nc.dram_tensor("wd_scr", [HC, 128, DM], BF16, kind=scr).ap()
    bo_scr = nc.dram_tensor("bo_scr", [1, DM], BF16, kind=scr).ap()
    if debug:
        dbg_o = nc.dram_tensor("dbg_o", [128, QT * DM], F32, kind="ExternalOutput").ap()
        dbg_y = nc.dram_tensor("dbg_y", [128, QT * DM], F32, kind="ExternalOutput").ap()
        dbg_l = nc.dram_tensor("dbg_l", [128, QT], F32, kind="ExternalOutput").ap()

    ctx = contextlib.ExitStack()

    gsb = lambda n, s, d: ctx.enter_context(nc.sbuf_tensor(n, s, d))
    ident = gsb("g_ident", [128, 128], BF16)
    ones = gsb("g_ones", [128, 512], BF16)
    consts = gsb("g_consts", [128, 4], F32)
    wo_bf = gsb("g_wo_bf", [128, DC * DM], BF16)
    dump = gsb("g_dump", [128, 1024], F32)
    eTMP = gsb("g_etmp", [128, 1024], F32)
    ps_tr0 = ctx.enter_context(nc.psum_tensor("ps_tr0", [128, 512], BF16))

    cA = contextlib.ExitStack()
    sbA = lambda n, s, d: cA.enter_context(nc.sbuf_tensor(n, s, d))
    ptA = lambda n, s, d: cA.enter_context(nc.psum_tensor(n, s, d))
    bqc = sbA("a_bqc", [128, DC], F32)
    bkc = sbA("a_bkc", [128, DC], F32)
    bor = sbA("a_bor", [1, DM], F32)
    bvc = sbA("a_bvc", [128, DC], F32)
    bvb = sbA("a_bvb", [128, DC], BF16)
    boP = sbA("a_boP", [1, DM], BF16)
    xg = [sbA(f"a_xg{i}", [128, 4 * DM], F32) for i in range(2)]
    xln = [sbA(f"a_xln{i}", [128, 4 * DM], BF16) for i in range(2)]
    xlnT = [sbA(f"a_xlnT{i}", [128, DC * 512], BF16) for i in range(2)]
    wk_bf = sbA("a_wk_bf", [128, DC * DM], BF16)
    wv_bf = sbA("a_wv_bf", [128, DC * DM], BF16)
    wq_bf = sbA("a_wq_bf", [128, DC * DM], BF16)
    wstage = [sbA(f"a_wstage{i}", [128, DM], F32) for i in range(2)]
    istage = [sbA(f"a_istage{i}", [128, HID // 2], I32) for i in range(4)]
    bstage = [sbA(f"a_bstage{i}", [128, HID // 2], BF16) for i in range(5)]
    kstage = [sbA(f"a_kstage{i}", [128, 512], BF16) for i in range(4)]
    qstage = [sbA(f"a_qstage{i}", [128, 512], BF16) for i in range(4)]
    vstage = [sbA(f"a_vstage{i}", [128, DM], BF16) for i in range(4)]
    amu = sbA("a_amu", [128, 8], F32)
    anmr = sbA("a_anmr", [128, 8], F32)
    arstd = sbA("a_arstd", [128, 8], F32)
    asum = sbA("a_asum", [128, 8], F32)
    asq = sbA("a_asq", [128, 8], F32)
    avar = sbA("a_avar", [128, 4], F32)
    alnv = sbA("a_alnv", [128, 4], F32)
    alnv2 = sbA("a_alnv2", [128, 4], F32)
    a_tr1 = ptA("a_tr1", [128, 512], BF16)
    ps_k = [ptA(f"a_ps_k{i}", [128, 512], F32) for i in range(2)]
    ps_v = [ptA(f"a_ps_v{i}", [128, 512], F32) for i in range(2)]
    ps_q = [ptA(f"a_ps_q{i}", [128, 512], F32) for i in range(2)]
    cA.close()

    cB = contextlib.ExitStack()
    sbB = lambda n, s, d: cB.enter_context(nc.sbuf_tensor(n, s, d))
    ptB = lambda n, s, d: cB.enter_context(nc.psum_tensor(n, s, d))
    boB = sbB("b_boB", [128, DM], BF16)
    qT_sb = sbB("b_qT", [128, DC * BQ], BF16)
    kT_sb = [sbB(f"b_kT{i}", [128, DC * 512], BF16) for i in range(2)]
    v_sb = [sbB(f"b_v{i}", [128, KCS * DM], BF16) for i in range(2)]
    o_acc = sbB("b_oacc", [128, QT * DM], F32)
    PT = sbB("b_PT", [128, 2 * KCS * BQ], BF16)
    ta = sbB("b_ta", [128, 2 * BQ], F32)
    tb = sbB("b_tb", [128, 2 * BQ], F32)
    onb = sbB("b_onb", [128, DM], BF16)
    onT = sbB("b_onT", [128, DM], BF16)
    y_sb = sbB("b_y", [128, QT * DM], BF16)
    xo_sb = sbB("b_xo", [128, DM], F32)
    x2st = sbB("b_x2st", [128, DM], BF16)
    linv = sbB("b_linv", [128, QT], F32)
    l_acc = sbB("b_lacc", [128, QT], F32)
    ysum = sbB("b_ysum", [128, QT], F32)
    yss = sbB("b_yss", [128, QT], F32)
    ymu = sbB("b_ymu", [128, QT], F32)
    yvar = sbB("b_yvar", [128, QT], F32)
    ylnv = sbB("b_ylnv", [128, QT], F32)
    ylnv2 = sbB("b_ylnv2", [128, QT], F32)
    yrstd = sbB("b_yrstd", [128, QT], F32)
    ynmr = sbB("b_ynmr", [128, QT], F32)
    ps_s = ptB("b_ps_s", [128, BQ], F32)
    ps_o = [ptB(f"b_ps_o{i}", [128, DM], F32) for i in range(2)]
    ps_l = ptB("b_ps_l", [128, QT], F32)
    cB.close()

    cC = contextlib.ExitStack()
    sbC = lambda n, s, d: cC.enter_context(nc.sbuf_tensor(n, s, d))
    ptC = lambda n, s, d: cC.enter_context(nc.psum_tensor(n, s, d))
    wg_sb = sbC("c_wg", [128, DC * HID], BF16)
    wu_sb = sbC("c_wu", [128, DC * HID], BF16)
    wdsup = [sbC(f"c_wd{i}", [128, HSUP * DM], BF16) for i in range(2)]
    x2g = [sbC(f"c_x2g{i}", [128, CT * DM], BF16) for i in range(2)]
    x2ln = sbC("c_x2ln", [128, CT * DM], BF16)
    x2lnT = sbC("c_x2lnT", [128, DC * CG], BF16)
    hT = sbC("c_hT", [128, 2 * HSUP * 512], BF16)
    g_sb = [sbC(f"c_g{i}", [128, 512], F32) for i in range(2)]
    y2acc = sbC("c_y2acc", [128, CT * DM], F32)
    outst = [sbC(f"c_outst{i}", [128, DM], F32) for i in range(2)]
    cmu = sbC("c_cmu", [128, 8], F32)
    cnmr = sbC("c_cnmr", [128, 8], F32)
    crstd = sbC("c_crstd", [128, 8], F32)
    csum = sbC("c_csum", [128, 8], F32)
    csq = sbC("c_csq", [128, 8], F32)
    cvar = sbC("c_cvar", [128, 4], F32)
    clnv = sbC("c_clnv", [128, 4], F32)
    clnv2 = sbC("c_clnv2", [128, 4], F32)
    y2sum = sbC("c_y2sum", [128, CT], F32)
    y2ss = sbC("c_y2ss", [128, CT], F32)
    y2mu = sbC("c_y2mu", [128, CT], F32)
    y2var = sbC("c_y2var", [128, CT], F32)
    y2lnv = sbC("c_y2lnv", [128, CT], F32)
    y2lnv2 = sbC("c_y2lnv2", [128, CT], F32)
    y2rstd = sbC("c_y2rstd", [128, CT], F32)
    y2nmr = sbC("c_y2nmr", [128, CT], F32)
    c_tr1 = ptC("c_tr1", [128, 512], BF16)
    ps_g = [ptC(f"c_ps_g{i}", [128, 512], F32) for i in range(2)]
    ps_u = [ptC(f"c_ps_u{i}", [128, 512], F32) for i in range(2)]
    ps_y2 = ptC("c_ps_y2", [128, DM], F32)
    cC.close()

    mks = lambda n, step=1: Tok(ctx.enter_context(nc.semaphore(n)), step)
    sPRE = mks("sPRE", 16)
    sWL = [mks(f"sWL{i}", 16) for i in range(2)]
    sIL = [mks(f"sIL{i}", 16) for i in range(4)]
    sXA = [mks(f"sXA{i}", 16) for i in range(2)]
    sKT = [mks(f"sKT{i}", 16) for i in range(2)]
    sV = [mks(f"sV{i}", 16) for i in range(2)]
    sQT = mks("sQT", 16)
    sXO = mks("sXO", 16)
    sBOB = mks("sBOB", 16)
    sCW = mks("sCW", 16)
    sCW2 = mks("sCW2", 16)
    sCW3 = mks("sCW3", 16)
    sCX2 = mks("sCX2", 16)
    sWD = [mks(f"sWD{i}", 16) for i in range(2)]
    vWC = mks("vWC")
    vLN = mks("vLN")
    vNRM = mks("vNRM")
    vTRC = mks("vTRC")
    vEV = mks("vEV")
    vTH = mks("vTH")
    vOE = mks("vOE")
    vLR = mks("vLR")
    vLA = mks("vLA")
    vON = mks("vON")
    vYE = mks("vYE")
    vX2 = mks("vX2")
    vHT = mks("vHT")
    vY2A = mks("vY2A")
    vBO = mks("vBO")
    vSELF = SelfSem(ctx.enter_context(nc.semaphore("vSELF")))
    sST = mks("sST")
    sRS = mks("sRS")
    sKQ = mks("sKQ")
    sTH = mks("sTH")
    sPT = mks("sPT")
    sSIL = mks("sSIL")
    sSELF = SelfSem(ctx.enter_context(nc.semaphore("sSELF")))
    tTR = mks("tTR")
    tPRJ = mks("tPRJ")
    tBO = mks("tBO")
    tSMM = mks("tSMM")
    tPV = mks("tPV")
    tYMM = mks("tYMM")
    tGUP = mks("tGUP")
    tDMM = mks("tDMM")
    gDQ = mks("gDQ")
    gDS = [mks(f"gDS{i}", 16) for i in range(2)]
    gKS = [mks(f"gKS{i}", 16) for i in range(4)]
    gQS = [mks(f"gQS{i}", 16) for i in range(4)]
    gVS = [mks(f"gVS{i}", 16) for i in range(4)]
    gBO = mks("gBO", 16)
    gX2 = mks("gX2", 16)
    gOS = [mks(f"gOS{i}", 16) for i in range(2)]

    MISC_N = 7
    pre_w = lambda i: MISC_N + i + 1
    pre_i = lambda j: MISC_N + NW + j + 1
    PRJ_OWN, PRJ_OTH = DC + 8 + DC, DC + 8

    def prj_base(g):
        return PRJ_OWN * min(g, NOG) + PRJ_OTH * max(0, g - NOG)

    prj_k = lambda g, ec: prj_base(g) + ec + 1
    prj_v = lambda g, vj: prj_base(g) + DC + vj + 1
    prj_q = lambda g, ec: prj_base(g) + DC + 8 + ec + 1
    PRJ_END = prj_base(NAG)

    def kq_base(g):
        return 2 * DC * min(g, NOG) + DC * max(0, g - NOG)

    kq_k = lambda g, ec: kq_base(g) + ec + 1
    kq_q = lambda g, ec: kq_base(g) + DC + ec + 1
    KQ_END = kq_base(NAG)
    kq_k_glob = lambda K: kq_k(K // DC, K % DC)
    kq_q_glob = lambda K: kq_q(K // DC, K % DC)
    vev = lambda g, vj: 8 * g + vj + 1

    ttr_a = lambda g, ec: DC * g + ec + 1
    TTR_A = DC * NAG
    ttr_b = lambda gq, qt, w: TTR_A + 2 * (QT * gq + qt) + w + 1
    TTR_B = TTR_A + 2 * QT * NBG
    ttr_c = lambda gc, ec: TTR_B + DC * gc + ec + 1
    vnrm_a = lambda g, i: 4 * g + i + 1
    VNRM_A = 4 * NAG
    vnrm_c = lambda gc, i: VNRM_A + 4 * gc + i + 1
    sst_a = lambda g: 8 * g + 8
    SST_A = 8 * NAG
    sst_b = lambda gq, qt: SST_A + QT * gq + qt + 1
    SST_B = SST_A + QT * NBG
    sst_cx = lambda gc: SST_B + (8 if gc == 0 else 16 * gc)
    sst_cy = lambda gc, ct: (SST_B + 16 * gc
                             + (16 if gc + 1 < NCG else 8) + 2 * (ct + 1))
    vln_a = lambda g: g + 1
    VLN_A = NAG
    vln_b = lambda gq: VLN_A + gq + 1
    VLN_B = VLN_A + NBG
    vln_cx = lambda gc: VLN_B + (1 if gc == 0 else 2 * gc)
    vln_cy = lambda gc: VLN_B + 2 * gc + (3 if gc + 1 < NCG else 2)
    srs_a = lambda g: g + 1
    SRS_A = NAG
    srs_b = lambda gq: SRS_A + gq + 1
    SRS_B = SRS_A + NBG
    srs_cx = lambda gc: SRS_B + (1 if gc == 0 else 2 * gc)
    srs_cy = lambda gc: SRS_B + 2 * gc + (3 if gc + 1 < NCG else 2)
    kcg_of = lambda gq, sc, kc: (gq * NSC + sc) * KCS + kc
    tpv = lambda scg, qt: QT * scg + qt + 1
    e_qt = lambda gq, qt: QT * gq + qt + 1
    VX2_B = QT * NBG
    vx2_c = lambda gc, ct: VX2_B + CT * gc + ct + 1
    cgup = lambda gc, hc: HC * gc + hc + 1
    csil = lambda gc, hc: HC * gc + hc + 1
    cht = lambda gc, hc: HC * gc + hc + 1
    cdmm = lambda gc, hcs, qt, half: ((gc * NHS + hcs) * CT + qt) * 2 + half + 1

    def a_xsrc(g):
        if g < NOG:
            return x_own[g * AG:(g + 1) * AG, :]
        return x_oth[(g - NOG) * AG:(g + 1 - NOG) * AG, :]

    def dq_parts(u):
        if u < 2 * DC:
            j, h = u // 2, u % 2
            return (wg_i[j * 128:(j + 1) * 128, h * HH:(h + 1) * HH],
                    wg_scr[j][:, h * HH:(h + 1) * HH], HH)
        if u < 4 * DC:
            uu = u - 2 * DC
            j, h = uu // 2, uu % 2
            return (wu_i[j * 128:(j + 1) * 128, h * HH:(h + 1) * HH],
                    wu_scr[j][:, h * HH:(h + 1) * HH], HH)
        j = u - 4 * DC
        return wd_i[j * 128:(j + 1) * 128, :], wd_scr[j], DM

    blk = ctx.enter_context(nc.Block())

    @blk.sync
    def _(sync):
        def dma(out, in_, tok, expect=None):
            ins = sync.dma_start(out=out, in_=in_)
            tok.inc(ins, expect)

        def xload(g):
            sXA[g % 2].wait(sync, g // 2)
            vNRM.wait(sync, 4 * (g - 2) + 4 if g >= 2 else 0)
            dma(xg[g % 2][:].rearrange("p (i d) -> p i d", i=4),
                a_xsrc(g).rearrange("(i p) d -> p i d", p=128), sXA[g % 2], g // 2 + 1)

        xload(0)
        xload(1)
        dma(ident[:], ident_h, sPRE, 1)
        dma(ones[:], ones_h, sPRE, 2)
        dma(consts[:], consts_h, sPRE, 3)
        dma(bqc[:], bq_c, sPRE, 4)
        dma(bkc[:], bk_c, sPRE, 5)
        dma(bor[:], bo_r, sPRE, 6)
        dma(bvc[:], bv_c, sPRE, 7)
        wsrc = [wk_t, wv_t, wq_t, wo_t]
        sint = [0]

        def int_load():
            u = sint[0]
            if u >= NU:
                return
            sint[0] += 1
            src_, _, w = dq_parts(u)
            sIL[u % 4].wait(sync, u // 4)
            gDQ.wait(sync, u - 3)
            sIL[u % 4].inc(sync.dma_start(out=istage[u % 4][:, 0:w], in_=src_),
                           u // 4 + 1)

        for i in range(NW):
            sWL[i % 2].wait(sync, i // 2)
            vWC.wait(sync, i - 1)
            dma(wstage[i % 2][:], wsrc[i // DC][(i % DC) * 128:(i % DC + 1) * 128, :],
                sWL[i % 2], i // 2 + 1)
            if i in (23, 27, 29, 30, 31):
                int_load()

        for g in range(2, NAG):
            xload(g)
            for _t in range(5):
                int_load()
        while sint[0] < NU:
            int_load()

        if phases < 2:
            return
        for p in range(4):
            gKS[p].wait(sync, DC * NAG // 4)
            gVS[p].wait(sync, 4 * NAG // 4)
            gQS[p].wait(sync, DC * NOG // 4)
        for p in range(2):
            gDS[p].wait(sync, NU // 2)
        if bsub >= 5:
            gBO.wait(sync, 1)
            dma(boB[:], bo_scr.partition_broadcast(128), sBOB, 1)
        v_reader = {}
        for gq in range(NBG):
            sQT.wait(sync, DC * gq)
            tSMM.wait(sync, KCS * NSC * gq)
            for ec in range(DC):
                Kq = DC * (2 * gq + 1) + ec
                gQS[Kq % 4].wait(sync, Kq // 4 + 1)
                dma(qT_sb[:, ec * BQ:(ec + 1) * BQ],
                    qT_scr[ec, :, gq * BQ:(gq + 1) * BQ], sQT, DC * gq + ec + 1)
            if bsub >= 6 and gq >= 1:
                for qt in range(QT):
                    t = QT * (gq - 1) + qt
                    sXO.wait(sync, t)
                    vX2.wait(sync, t)
                    dma(xo_sb[:], x_own[t * 128:(t + 1) * 128, :], sXO, t + 1)
            for sc in range(NSC):
                scg = gq * NSC + sc
                if gq >= 1 and sc == 0:
                    continue
                sd = sc if gq % 2 == 0 else NSC - 1 - sc
                lidx = scg - gq
                for h in range(2):
                    hsc = 2 * scg + h
                    sKT[h].wait(sync, DC * lidx)
                    tSMM.wait(sync, 4 * (hsc - 1))
                    for ec in range(DC):
                        if gq == 0:
                            Kk = DC * (2 * sd + h) + ec
                            gKS[Kk % 4].wait(sync, Kk // 4 + 1)
                        dma(kT_sb[h][:, ec * 512:(ec + 1) * 512],
                            kT_scr[ec, :, sd * SCn + h * 512:sd * SCn + h * 512 + 512],
                            sKT[h], DC * lidx + ec + 1)
                sV[lidx % 2].wait(sync, lidx // 2)
                if lidx % 2 in v_reader:
                    tPV.wait(sync, QT * (v_reader[lidx % 2] + 1))
                if gq == 0:
                    Vv = 4 * (2 * sd + 1) + 3
                    gVS[Vv % 4].wait(sync, Vv // 4 + 1)
                dma(v_sb[lidx % 2][:].rearrange("p (c e) -> p c e", c=KCS),
                    v_scr[sd * KCS:(sd + 1) * KCS].rearrange("c p e -> p c e"),
                    sV[lidx % 2], lidx // 2 + 1)
                nxt = scg + 1
                v_reader[lidx % 2] = scg + (1 if (nxt % NSC == 0 and
                                                  nxt < NSC * NBG) else 0)
        if bsub >= 6:
            for qt in range(QT):
                t = QT * (NBG - 1) + qt
                sXO.wait(sync, t)
                vX2.wait(sync, t)
                dma(xo_sb[:], x_own[t * 128:(t + 1) * 128, :], sXO, t + 1)

        if phases < 3:
            return
        tYMM.wait(sync, QT * NBG)
        vX2.wait(sync, VX2_B)
        sST.wait(sync, SST_B)
        gX2.wait(sync, QT * NBG)
        gX2.wait(sync, CT)
        dma(x2g[0][:].rearrange("p (i d) -> p i d", i=CT),
            x2_scr[0:CG, :].rearrange("(i p) d -> p i d", p=128), sCX2, 1)
        for dc in range(DC // 2):
            for p in range(2):
                gDS[p].wait(sync, dc + 1)
            dma(wg_sb[:, dc * HID:(dc + 1) * HID], wg_scr[dc], sCW, dc + 1)
        sCW.wait(sync, DC // 2)
        for hcs in range(NHS):
            sWD[hcs % 2].wait(sync, hcs // 2)
            if hcs >= 2:
                tDMM.wait(sync, (hcs - 1) * CT * 2)
            jmax = 4 * DC + hcs * HSUP + HSUP - 1
            gDS[jmax % 2].wait(sync, jmax // 2 + 1)
            gDS[(jmax - 1) % 2].wait(sync, (jmax - 1) // 2 + 1)
            dma(wdsup[hcs % 2][:].rearrange("p (h d) -> p h d", h=HSUP),
                wd_scr[hcs * HSUP:(hcs + 1) * HSUP].rearrange("h p d -> p h d"),
                sWD[hcs % 2], hcs // 2 + 1)
        for gc in range(1, NCG):
            sCX2.wait(sync, gc - 1)
            gX2.wait(sync, CT * gc + CT)
            if gc >= 2:
                vX2.wait(sync, vx2_c(gc - 2, CT - 1))
            dma(x2g[gc % 2][:].rearrange("p (i d) -> p i d", i=CT),
                x2_scr[gc * CG:(gc + 1) * CG, :].rearrange("(i p) d -> p i d", p=128),
                sCX2, gc + 1)
            for hcs in range(NHS):
                ghcs = gc * NHS + hcs
                sWD[ghcs % 2].wait(sync, ghcs // 2)
                if ghcs >= 2:
                    tDMM.wait(sync, (ghcs - 1) * CT * 2)
                jmax = 4 * DC + hcs * HSUP + HSUP - 1
                gDS[jmax % 2].wait(sync, jmax // 2 + 1)
                gDS[(jmax - 1) % 2].wait(sync, (jmax - 1) // 2 + 1)
                dma(wdsup[ghcs % 2][:].rearrange("p (h d) -> p h d", h=HSUP),
                    wd_scr[hcs * HSUP:(hcs + 1) * HSUP].rearrange("h p d -> p h d"),
                    sWD[ghcs % 2], ghcs // 2 + 1)

    @blk.gpsimd
    def _(gp):
        for g in range(NAG):
            own = g < NOG
            if asub >= 3:
                for ec in range(DC):
                    K = DC * g + ec
                    gKS[K % 4].wait(gp, K // 4)
                    sKQ.wait(gp, kq_k(g, ec))
                    gKS[K % 4].inc(gp.dma_start(out=kT_scr[ec, :, g * AG:(g + 1) * AG],
                                                in_=kstage[ec % 4][:]), K // 4 + 1)
            if asub >= 3:
                for i in range(4):
                    V = 4 * g + i
                    gVS[V % 4].wait(gp, V // 4)
                    vEV.wait(gp, vev(g, 2 * i + 1))
                    gVS[V % 4].inc(gp.dma_start(out=v_scr[4 * g + i],
                                                in_=vstage[i % 4][:]), V // 4 + 1)
                if own:
                    for ec in range(DC):
                        K = DC * g + ec
                        gQS[K % 4].wait(gp, K // 4)
                        sKQ.wait(gp, kq_q(g, ec))
                        gQS[K % 4].inc(gp.dma_start(out=qT_scr[ec, :, g * AG:(g + 1) * AG],
                                                    in_=qstage[ec % 4][:]), K // 4 + 1)
        if phases < 2:
            for p in range(2):
                gp.wait_ge(gDS[p].h, 16 * NU // 2)
            if asub == 1:
                gp.wait_ge(vNRM.h, 4 * NAG)
            elif asub == 2:
                gp.wait_ge(vTRC.h, DC * NAG)
            else:
                for p in range(4):
                    gp.wait_ge(gKS[p].h, 16 * DC * NAG // 4)
                    gp.wait_ge(gVS[p].h, 16 * 4 * NAG // 4)
                    gp.wait_ge(gQS[p].h, 16 * DC * NOG // 4)
            return
        if bsub < 6:
            if bsub == 1:
                gp.wait_ge(sPT.h, KCS * NSC * NBG)
            elif bsub == 2:
                gp.wait_ge(vOE.h, QT * NSC * NBG)
                gp.wait_ge(vLA.h, NSC * NBG)
            elif bsub == 3:
                import os as _os
                gp.wait_ge(vON.h, int(_os.environ.get("BQT", QT)) * NBG)
            elif bsub == 4:
                gp.wait_ge(vTRC.h, TTR_A + 2 * QT * NBG)
            else:
                gp.wait_ge(sST.h, SST_B)
            if bsub == 5:
                vBO.wait(gp, 1)
                gBO.inc(gp.dma_start(out=bo_scr, in_=boP[:]), 1)
            return
        vBO.wait(gp, 1)
        gBO.inc(gp.dma_start(out=bo_scr, in_=boP[:]), 1)
        for gq in range(NBG):
            for qt in range(QT):
                t = QT * gq + qt
                gX2.wait(gp, t)
                vX2.wait(gp, t + 1)
                gX2.inc(gp.dma_start(out=x2_scr[t * 128:(t + 1) * 128, :],
                                     in_=x2st[:]), t + 1)
        if phases >= 3:
            tYMM.wait(gp, QT * NBG)
            vX2.wait(gp, VX2_B)
            sST.wait(gp, SST_B)
            gp.wait_ge(gX2.h, 16 * QT * NBG)
            for i in range(DC // 2):
                dc = DC // 2 + i
                for p in range(2):
                    gDS[p].wait(gp, dc + 1)
                sCW3.inc(gp.dma_start(out=wg_sb[:, dc * HID:(dc + 1) * HID],
                                      in_=wg_scr[dc]), i + 1)
            sCW3.wait(gp, DC // 2)
        if phases < 3:
            gp.wait_ge(gX2.h, 16 * QT * NBG)
            if debug:
                gp.dma_start(out=dbg_o, in_=o_acc[:]).then_inc(gOS.h, 16)
                gp.dma_start(out=dbg_y, in_=y_sb[:]).then_inc(gOS.h, 16)
                gp.dma_start(out=dbg_l, in_=linv[:]).then_inc(gOS.h, 16)
                gp.wait_ge(gOS.h, 48)
            return
        for gc in range(NCG):
            for ct in range(CT):
                t = CT * gc + ct
                gOS[t % 2].wait(gp, t // 2)
                vX2.wait(gp, VX2_B + t + 1)
                gOS[t % 2].inc(gp.dma_start(out=y_out[t * 128:(t + 1) * 128, :],
                                            in_=outst[t % 2][:]), t // 2 + 1)
        for p in range(2):
            gp.wait_ge(gOS[p].h, 16 * CT * NCG // 2)

    @blk.scalar
    def _(sc_):
        def S(ins, tok=None, expect=None):
            if tok is None:
                sSELF.n += 1
                ins.then_inc(sSELF.h, 1)
                sc_.wait_ge(sSELF.h, sSELF.n)
            else:
                tok.inc(ins, expect)
                sc_.wait_ge(tok.h, tok.n)
            return ins

        sPRE.wait(sc_, MISC_N)
        sstore = [0]

        def bf_store():
            u = sstore[0]
            if u >= NU:
                return
            sstore[0] += 1
            _, dst, w = dq_parts(u)
            gDS[u % 2].wait(sc_, u // 2)
            gDQ.wait(sc_, u + 1)
            gDS[u % 2].inc(sc_.dma_start(out=dst, in_=bstage[u % 5][:, 0:w]),
                           u // 2 + 1)

        def a_stats(g):
            p4 = (g % 2) * 4
            sXA[g % 2].wait(sc_, g // 2 + 1)
            for i in range(4):
                S(sc_.activation(dump[:], xg[g % 2][:, i * DM:(i + 1) * DM], AF.Copy,
                                 accum_out=asum[:, p4 + i:p4 + i + 1]), sST, 8 * g + i + 1)
            for i in range(4):
                S(sc_.activation(dump[:], xg[g % 2][:, i * DM:(i + 1) * DM], AF.Square,
                                 accum_out=asq[:, p4 + i:p4 + i + 1]), sST, 8 * g + 4 + i + 1)

        def a_rstd(g):
            p4 = (g % 2) * 4
            vLN.wait(sc_, vln_a(g))
            S(sc_.activation(alnv[:], avar[:], AF.Ln, bias=consts[:, 1:2]))
            S(sc_.activation(arstd[:, p4:p4 + 4], alnv[:], AF.Exp, scale=-0.5),
              sRS, srs_a(g))

        a_stats(0)
        a_rstd(0)
        for g in range(NAG):
            own = g < NOG
            if asub >= 3:
                for ec in range(DC):
                    tPRJ.wait(sc_, prj_k(g, ec))
                    K = DC * g + ec
                    gKS[K % 4].wait(sc_, K // 4)
                    S(sc_.activation(kstage[ec % 4][:], ps_k[ec % 2][:], AF.Identity,
                                     bias=bkc[:, ec:ec + 1]), sKQ, kq_k(g, ec))
                if g + 1 < NAG:
                    a_stats(g + 1)
                if own:
                    for ec in range(DC):
                        tPRJ.wait(sc_, prj_q(g, ec))
                        K = DC * g + ec
                        gQS[K % 4].wait(sc_, K // 4)
                        S(sc_.activation(qstage[ec % 4][:], ps_q[ec % 2][:], AF.Identity,
                                         bias=bqc[:, ec:ec + 1]), sKQ, kq_q(g, ec))
                if g + 1 < NAG:
                    a_rstd(g + 1)
                if g >= 1:
                    for _t in range(5):
                        bf_store()
            elif g + 1 < NAG:
                a_stats(g + 1)
                a_rstd(g + 1)
        while sstore[0] < NU:
            bf_store()
        if phases < 2:
            return
        for gq in range(NBG):
            for sc in range(NSC):
                scg = gq * NSC + sc
                for kc in range(KCS):
                    kcg = kcg_of(gq, sc, kc)
                    hh = (kcg % 2) * BQ
                    tSMM.wait(sc_, kcg + 1)
                    vTH.wait(sc_, 2 * kcg - 2)
                    S(sc_.activation(ta[:, hh:hh + BQ], ps_s[:], AF.Tanh, scale=0.5),
                      sTH, 2 * kcg + 1)
                    S(sc_.activation(tb[:, hh:hh + BQ], ps_s[:], AF.Tanh),
                      sTH, 2 * kcg + 2)
                    vTH.wait(sc_, 2 * kcg + 2)
                    if kc == 0 and bsub >= 2:
                        tPV.wait(sc_, QT * (scg - 1))
                    psl = (scg % 2) * KCS * BQ
                    S(sc_.activation(PT[:, psl + kc * BQ:psl + (kc + 1) * BQ],
                                     tb[:, hh:hh + BQ],
                                     AF.Exp, scale=0.1, bias=consts[:, 0:1]),
                      sPT, kcg + 1)
            if bsub < 5:
                continue
            for qt in range(QT):
                vYE.wait(sc_, e_qt(gq, qt))
                S(sc_.activation(dump[:], y_sb[:, qt * DM:(qt + 1) * DM], AF.Square,
                                 accum_out=yss[:, qt:qt + 1]), sST, sst_b(gq, qt))
            vLN.wait(sc_, vln_b(gq))
            S(sc_.activation(ylnv[:], yvar[:], AF.Ln, bias=consts[:, 1:2]))
            S(sc_.activation(yrstd[:], ylnv[:], AF.Exp, scale=-0.5), sRS, srs_b(gq))
        if phases < 3:
            return
        tYMM.wait(sc_, QT * NBG)
        vX2.wait(sc_, VX2_B)
        gX2.wait(sc_, QT * NBG)
        for dc in range(DC):
            for p in range(2):
                gDS[p].wait(sc_, DC + dc + 1)
            sCW2.inc(sc_.dma_start(out=wu_sb[:, dc * HID:(dc + 1) * HID],
                                   in_=wu_scr[dc]), dc + 1)
        def c_stats(gc):
            p4 = (gc % 2) * 4
            b = SST_B + (0 if gc == 0 else 16 * gc - 8)
            sCX2.wait(sc_, gc + 1)
            for i in range(CT):
                S(sc_.activation(dump[:], x2g[gc % 2][:, i * DM:(i + 1) * DM], AF.Copy,
                                 accum_out=csum[:, p4 + i:p4 + i + 1]), sST, b + i + 1)
            for i in range(CT):
                S(sc_.activation(dump[:], x2g[gc % 2][:, i * DM:(i + 1) * DM], AF.Square,
                                 accum_out=csq[:, p4 + i:p4 + i + 1]), sST, b + 4 + i + 1)

        def c_rstd(gc):
            p4 = (gc % 2) * 4
            vLN.wait(sc_, vln_cx(gc))
            S(sc_.activation(clnv[:], cvar[:], AF.Ln, bias=consts[:, 1:2]))
            S(sc_.activation(crstd[:, p4:p4 + 4], clnv[:], AF.Exp, scale=-0.5),
              sRS, srs_cx(gc))

        c_stats(0)
        c_rstd(0)
        for gc in range(NCG):
            if gc == 0:
                sCW2.wait(sc_, DC)
            for hc in range(HC):
                tGUP.wait(sc_, cgup(gc, hc))
                S(sc_.activation(g_sb[hc % 2][:], ps_g[hc % 2][:], AF.Silu),
                  sSIL, csil(gc, hc))
            if gc + 1 < NCG:
                c_stats(gc + 1)
                c_rstd(gc + 1)
            for ct in range(CT):
                vY2A.wait(sc_, cdmm(gc, NHS - 1, ct, 1))
                S(sc_.activation(dump[:], y2acc[:, ct * DM:(ct + 1) * DM], AF.Copy,
                                 accum_out=y2sum[:, ct:ct + 1]), sST, sst_cy(gc, ct) - 1)
                S(sc_.activation(dump[:], y2acc[:, ct * DM:(ct + 1) * DM], AF.Square,
                                 accum_out=y2ss[:, ct:ct + 1]), sST, sst_cy(gc, ct))
            vLN.wait(sc_, vln_cy(gc))
            S(sc_.activation(y2lnv[:], y2var[:], AF.Ln, bias=consts[:, 1:2]))
            S(sc_.activation(y2rstd[:], y2lnv[:], AF.Exp, scale=-0.5), sRS, srs_cy(gc))

    @blk.vector
    def _(ve):
        def V(ins, tok=None, expect=None):
            if tok is None:
                vSELF.n += 1
                ins.then_inc(vSELF.h, 1)
                ve.wait_ge(vSELF.h, vSELF.n)
            else:
                tok.inc(ins, expect)
                ve.wait_ge(tok.h, tok.n)
            return ins

        wdst = [wk_bf, wv_bf, wq_bf, wo_bf]
        for i in range(NW):
            sWL[i % 2].wait(ve, i // 2 + 1)
            dc = i % DC
            V(ve.tensor_copy(wdst[i // DC][:, dc * DM:(dc + 1) * DM],
                             wstage[i % 2][:]), vWC, i + 1)
        sPRE.wait(ve, MISC_N)
        V(ve.tensor_copy(bvb[:], bvc[:]), vWC, NW + 1)
        ts_j = [0]

        def dequant_ts():
            u = ts_j[0]
            if u >= NU:
                return
            ts_j[0] += 1
            sIL[u % 4].wait(ve, u // 4 + 1)
            if u >= 5:
                gDS[(u - 5) % 2].wait(ve, (u - 5) // 2 + 1)
            w = HH if u < 4 * DC else DM
            V(ve.tensor_scalar(bstage[u % 5][:, 0:w], istage[u % 4][:, 0:w],
                               DEQ, None, ALU.mult), gDQ, u + 1)

        def a_statchain(g):
            p4 = (g % 2) * 4
            sST.wait(ve, sst_a(g))
            V(ve.tensor_scalar(amu[:, p4:p4 + 4], asum[:, p4:p4 + 4], INV_DM,
                               None, ALU.mult))
            V(ve.tensor_tensor(alnv2[:], amu[:, p4:p4 + 4], amu[:, p4:p4 + 4],
                               ALU.mult))
            V(ve.scalar_tensor_tensor(avar[:], asq[:, p4:p4 + 4], INV_DM, alnv2[:],
                                      ALU.mult, ALU.subtract), vLN, vln_a(g))

        def a_norms(g):
            p4 = (g % 2) * 4
            sRS.wait(ve, srs_a(g))
            V(ve.scalar_tensor_tensor(anmr[:, p4:p4 + 4], amu[:, p4:p4 + 4], -1.0,
                                      arstd[:, p4:p4 + 4], ALU.mult, ALU.mult))
            if g >= 2 and asub >= 2:
                tTR.wait(ve, ttr_a(g - 2, DC - 1))
            for i in range(4):
                V(ve.tensor_scalar(xln[g % 2][:, i * DM:(i + 1) * DM],
                                   xg[g % 2][:, i * DM:(i + 1) * DM],
                                   arstd[:, p4 + i:p4 + i + 1],
                                   anmr[:, p4 + i:p4 + i + 1], ALU.mult, ALU.add),
                  vNRM, vnrm_a(g, i))

        a_statchain(0)
        a_norms(0)
        for g in range(NAG):
            own = g < NOG
            p4 = (g % 2) * 4
            if asub < 2:
                if g + 1 < NAG:
                    a_statchain(g + 1)
                    a_norms(g + 1)
                continue
            if g >= 2 and asub >= 3:
                tPRJ.wait(ve, prj_base(g - 1))
            for ec in range(DC):
                tTR.wait(ve, ttr_a(g, ec))
                V(ve.tensor_copy(xlnT[g % 2][:, ec * 512:(ec + 1) * 512],
                                 [ps_tr0, a_tr1][ec % 2][:]),
                  vTRC, ttr_a(g, ec))
            if asub < 3:
                if g + 1 < NAG:
                    a_statchain(g + 1)
                    a_norms(g + 1)
                continue
            for i in range(4):
                V4 = 4 * g + i
                gVS[V4 % 4].wait(ve, V4 // 4)
                for h in range(2):
                    vj = 2 * i + h
                    tPRJ.wait(ve, prj_v(g, vj))
                    V(ve.tensor_copy(vstage[i % 4][:, h * 512:(h + 1) * 512],
                                     ps_v[vj % 2][:]), vEV, vev(g, vj))
                if g >= 1:
                    dequant_ts()
            if g >= 1:
                dequant_ts()
            if g + 1 < NAG:
                a_statchain(g + 1)
                a_norms(g + 1)
        while ts_j[0] < NU:
            dequant_ts()
        if phases < 2:
            return
        if bsub >= 5:
            tBO.wait(ve, 2)
            V(ve.tensor_tensor(boP[0:1, 0:512], ps_k[0][0:1, :], bor[0:1, 0:512],
                               ALU.add))
            V(ve.tensor_tensor(boP[0:1, 512:1024], ps_k[1][0:1, :],
                               bor[0:1, 512:1024], ALU.add), vBO, 1)
        def _epi_tail(gq, qt):
            for w in range(2):
                tTR.wait(ve, ttr_b(gq, qt, w))
                V(ve.tensor_copy(onT[:, w * 512:w * 512 + 512],
                                 [ps_tr0, a_tr1][w][:]),
                  vTRC, ttr_b(gq, qt, w))
            if bsub < 5:
                return
            tYMM.wait(ve, e_qt(gq, qt))
            if gq >= 1:
                sST.wait(ve, sst_b(gq - 1, qt))
            if gq == 0 and qt == 0:
                sBOB.wait(ve, 1)
            V(ve.scalar_tensor_tensor(y_sb[:, qt * DM:(qt + 1) * DM],
                                      ps_o[qt % 2][:], linv[:, qt:qt + 1],
                                      boB[:], ALU.mult, ALU.add,
                                      accum_out=ysum[:, qt:qt + 1]),
              vYE, e_qt(gq, qt))

        for gq in range(NBG):
            for slot in range(NSC + 1):
                for j in range(KCS):
                    if slot < NSC:
                        sc = slot
                        kc = j
                        kcg = kcg_of(gq, sc, kc)
                        hh = (kcg % 2) * BQ
                        sTH.wait(ve, 2 * kcg + 2)
                        V(ve.scalar_tensor_tensor(ta[:, hh:hh + BQ], ps_s[:], 0.0,
                                                  ta[:, hh:hh + BQ], ALU.max, ALU.add),
                          vTH, 2 * kcg + 1)
                        V(ve.scalar_tensor_tensor(tb[:, hh:hh + BQ], tb[:, hh:hh + BQ],
                                                  1.5, ta[:, hh:hh + BQ], ALU.mult,
                                                  ALU.add), vTH, 2 * kcg + 2)
                    if slot > 0 and bsub >= 2:
                        sc = slot - 1
                        scg = gq * NSC + sc
                        qt = j
                        tPV.wait(ve, tpv(scg, qt))
                        dst = o_acc[:, qt * DM:(qt + 1) * DM]
                        if sc == 0:
                            ins = ve.tensor_scalar(dst, ps_o[qt % 2][:], 1.0, None,
                                                   ALU.mult)
                        else:
                            ins = ve.tensor_tensor(dst, dst, ps_o[qt % 2][:], ALU.add)
                        V(ins, vOE, QT * scg + qt + 1)
                        if slot == NSC and bsub >= 3:
                            ins = ve.tensor_tensor(l_acc[:, qt:qt + 1],
                                                   l_acc[:, qt:qt + 1],
                                                   ps_l[:, qt:qt + 1], ALU.add)
                            if qt == QT - 1:
                                V(ins, vLA, scg + 1)
                            else:
                                V(ins)
                            V(ve.reciprocal(linv[:, qt:qt + 1], l_acc[:, qt:qt + 1]))
                            if qt >= 1 and bsub >= 4:
                                _epi_tail(gq, qt - 1)
                            if QT * gq + qt > 0 and bsub >= 4:
                                tTR.wait(ve, ttr_b(gq, qt, 0) - 1)
                            V(ve.tensor_scalar(onb[:], dst, 1.0, None, ALU.mult),
                              vON, e_qt(gq, qt))
                if slot > 0 and bsub >= 2 and slot < NSC:
                    sc = slot - 1
                    scg = gq * NSC + sc
                    if sc == 0:
                        ins = ve.tensor_scalar(l_acc[:], ps_l[:], 1.0, None, ALU.mult)
                    else:
                        ins = ve.tensor_tensor(l_acc[:], l_acc[:], ps_l[:], ALU.add)
                    V(ins, vLA, scg + 1)
            if bsub < 4:
                continue
            _epi_tail(gq, QT - 1)
            if bsub < 5:
                continue
            sST.wait(ve, sst_b(gq, QT - 1))
            V(ve.tensor_scalar(ymu[:], ysum[:], INV_DM, None, ALU.mult))
            V(ve.tensor_tensor(ylnv2[:], ymu[:], ymu[:], ALU.mult))
            V(ve.scalar_tensor_tensor(yvar[:], yss[:], INV_DM, ylnv2[:], ALU.mult,
                                      ALU.subtract), vLN, vln_b(gq))
            sRS.wait(ve, srs_b(gq))
            V(ve.scalar_tensor_tensor(ynmr[:], ymu[:], -1.0, yrstd[:], ALU.mult,
                                      ALU.mult))
            if bsub < 6:
                continue
            for qt in range(QT):
                gX2.wait(ve, QT * gq + qt)
                sXO.wait(ve, QT * gq + qt + 1)
                V(ve.tensor_scalar(eTMP[:], y_sb[:, qt * DM:(qt + 1) * DM],
                                   yrstd[:, qt:qt + 1],
                                   ynmr[:, qt:qt + 1], ALU.mult, ALU.add))
                V(ve.tensor_tensor(x2st[:], eTMP[:], xo_sb[:], ALU.add),
                  vX2, e_qt(gq, qt))
        if phases < 3:
            return
        def c_chain(gc):
            p4 = (gc % 2) * 4
            sST.wait(ve, sst_cx(gc))
            V(ve.tensor_scalar(cmu[:, p4:p4 + 4], csum[:, p4:p4 + 4], INV_DM,
                               None, ALU.mult))
            V(ve.tensor_tensor(clnv2[:], cmu[:, p4:p4 + 4], cmu[:, p4:p4 + 4],
                               ALU.mult))
            V(ve.scalar_tensor_tensor(cvar[:], csq[:, p4:p4 + 4], INV_DM, clnv2[:],
                                      ALU.mult, ALU.subtract), vLN, vln_cx(gc))

        def c_norms(gc):
            p4 = (gc % 2) * 4
            sRS.wait(ve, srs_cx(gc))
            V(ve.scalar_tensor_tensor(cnmr[:, p4:p4 + 4], cmu[:, p4:p4 + 4], -1.0,
                                      crstd[:, p4:p4 + 4], ALU.mult, ALU.mult))
            if gc >= 1:
                tTR.wait(ve, ttr_c(gc - 1, DC - 1))
            for i in range(CT):
                V(ve.tensor_scalar(x2ln[:, i * DM:(i + 1) * DM],
                                   x2g[gc % 2][:, i * DM:(i + 1) * DM],
                                   crstd[:, p4 + i:p4 + i + 1],
                                   cnmr[:, p4 + i:p4 + i + 1], ALU.mult, ALU.add),
                  vNRM, vnrm_c(gc, i))

        def c_copies(gc):
            if gc >= 1:
                tGUP.wait(ve, cgup(gc - 1, HC - 1))
            for ec in range(DC):
                tTR.wait(ve, ttr_c(gc, ec))
                V(ve.tensor_copy(x2lnT[:, ec * 512:(ec + 1) * 512],
                                 [ps_tr0, c_tr1][ec % 2][:]),
                  vTRC, ttr_c(gc, ec))

        c_chain(0)
        c_norms(0)
        c_copies(0)
        for gc in range(NCG):
            for hcs in range(NHS):
                for hc in range(hcs * HSUP, (hcs + 1) * HSUP):
                    ghs = gc * NHS + hc // HSUP
                    if ghs >= 2:
                        tDMM.wait(ve, (ghs - 1) * CT * 2)
                    sSIL.wait(ve, csil(gc, hc))
                    hsl = (hc % (2 * HSUP)) * 512
                    V(ve.tensor_tensor(hT[:, hsl:hsl + 512], ps_u[hc % 2][:],
                                       g_sb[hc % 2][:], ALU.mult), vHT, cht(gc, hc))
                if hcs == NHS - 1 and gc + 1 < NCG:
                    c_chain(gc + 1)
                    c_norms(gc + 1)
                for qt in range(CT):
                    for half in range(2):
                        tDMM.wait(ve, cdmm(gc, hcs, qt, half))
                        dst = y2acc[:, qt * DM + half * 512:qt * DM + half * 512 + 512]
                        if hcs == 0:
                            ins = ve.tensor_scalar(dst, ps_y2[:, half * 512:half * 512 + 512],
                                                   1.0, None, ALU.mult)
                        else:
                            ins = ve.tensor_tensor(dst, dst,
                                                   ps_y2[:, half * 512:half * 512 + 512],
                                                   ALU.add)
                        V(ins, vY2A, cdmm(gc, hcs, qt, half))
            if gc + 1 < NCG:
                c_copies(gc + 1)
            sST.wait(ve, sst_cy(gc, CT - 1))
            V(ve.tensor_scalar(y2mu[:], y2sum[:], INV_DM, None, ALU.mult))
            V(ve.tensor_tensor(y2lnv2[:], y2mu[:], y2mu[:], ALU.mult))
            V(ve.scalar_tensor_tensor(y2var[:], y2ss[:], INV_DM, y2lnv2[:],
                                      ALU.mult, ALU.subtract), vLN, vln_cy(gc))
            sRS.wait(ve, srs_cy(gc))
            V(ve.scalar_tensor_tensor(y2nmr[:], y2mu[:], -1.0, y2rstd[:], ALU.mult,
                                      ALU.mult))
            for ct in range(CT):
                t_o = CT * gc + ct
                gOS[t_o % 2].wait(ve, t_o // 2)
                V(ve.tensor_scalar(eTMP[:], y2acc[:, ct * DM:(ct + 1) * DM],
                                   y2rstd[:, ct:ct + 1], y2nmr[:, ct:ct + 1],
                                   ALU.mult, ALU.add))
                V(ve.tensor_tensor(outst[ct % 2][:], eTMP[:],
                                   x2g[gc % 2][:, ct * DM:(ct + 1) * DM], ALU.add),
                  vX2, vx2_c(gc, ct))

    @blk.tensor
    def _(te):
        sPRE.wait(te, MISC_N)
        for g in range(NAG):
            own = g < NOG
            if asub < 2:
                continue
            vNRM.wait(te, vnrm_a(g, 3))
            for ec in range(DC):
                if DC * g + ec >= 2:
                    vTRC.wait(te, DC * g + ec - 1)
                trb = [ps_tr0, a_tr1][ec % 2]
                last = None
                for i in range(4):
                    last = te.transpose(
                        trb[:, i * 128:i * 128 + 128],
                        xln[g % 2][:, i * DM + ec * 128:i * DM + ec * 128 + 128],
                        ident[:])
                tTR.inc(last, ttr_a(g, ec))
            if asub < 3:
                continue
            vTRC.wait(te, ttr_a(g, DC - 1))
            if g == 0:
                vWC.wait(te, DC)
            for ec in range(DC):
                K = DC * g + ec
                if K >= 2:
                    sKQ.wait(te, kq_k_glob(K - 2))
                last = None
                for dc in range(DC):
                    last = te.matmul(ps_k[ec % 2][:],
                                     wk_bf[:, dc * DM + ec * 128:dc * DM + ec * 128 + 128],
                                     xlnT[g % 2][:, dc * 512:(dc + 1) * 512],
                                     start=(dc == 0), stop=(dc == DC - 1))
                tPRJ.inc(last, prj_k(g, ec))
            if g == 0:
                vWC.wait(te, 2 * DC)
            for vj in range(8):
                i, h = vj // 2, vj % 2
                if 8 * g + vj >= 2:
                    vEV.wait(te, 8 * g + vj - 1)
                last = None
                for dc in range(DC):
                    last = te.matmul(ps_v[vj % 2][:],
                                     xlnT[g % 2][:, dc * 512 + i * 128:dc * 512 + i * 128 + 128],
                                     wv_bf[:, dc * DM + h * 512:dc * DM + h * 512 + 512],
                                     start=(dc == 0), stop=(dc == DC - 1))
                tPRJ.inc(last, prj_v(g, vj))
            if own:
                if g == 0:
                    vWC.wait(te, 3 * DC)
                for ec in range(DC):
                    K = DC * g + ec
                    if K >= 2:
                        sKQ.wait(te, kq_q_glob(K - 2))
                    last = None
                    for dc in range(DC):
                        last = te.matmul(ps_q[ec % 2][:],
                                         wq_bf[:, dc * DM + ec * 128:dc * DM + ec * 128 + 128],
                                         xlnT[g % 2][:, dc * 512:(dc + 1) * 512],
                                         start=(dc == 0), stop=(dc == DC - 1))
                    tPRJ.inc(last, prj_q(g, ec))
        if phases < 2:
            return
        if bsub >= 5:
            sKQ.wait(te, KQ_END)
            vWC.wait(te, NW + 1)
        if bsub >= 5:
            for half in range(2):
                last = None
                for dc in range(DC):
                    last = te.matmul(ps_k[half][0:1, :], bvb[:, dc:dc + 1],
                                     wo_bf[:, dc * DM + half * 512:dc * DM + half * 512 + 512],
                                     start=(dc == 0), stop=(dc == DC - 1))
                tBO.inc(last, half + 1)
        def _epi_pe(gq, qt):
            vON.wait(te, e_qt(gq, qt))
            if qt == 0:
                kl = KCS * NSC * (gq + 1)
                sTH.wait(te, 2 * kl)
                vTH.wait(te, 2 * kl - 1)
            for w in range(2):
                vTRC.wait(te, ttr_b(gq, qt, w) - 2)
                trb = [ps_tr0, a_tr1][w]
                last = None
                for dc in range(4 * w, 4 * w + 4):
                    last = te.transpose(trb[:, (dc % 4) * 128:(dc % 4) * 128 + 128],
                                        onb[:, dc * 128:(dc + 1) * 128], ident[:])
                tTR.inc(last, ttr_b(gq, qt, w))
            if bsub < 5:
                return
            vTRC.wait(te, ttr_b(gq, qt, 1))
            if qt >= 2:
                vYE.wait(te, e_qt(gq, qt) - 2)
            last = None
            for h in range(2):
                for dc in range(DC):
                    last = te.matmul(
                        ps_o[qt % 2][:, h * 512:h * 512 + 512],
                        onT[:, dc * 128:(dc + 1) * 128],
                        wo_bf[:, dc * DM + h * 512:dc * DM + h * 512 + 512],
                        start=(dc == 0), stop=(dc == DC - 1))
            tYMM.inc(last, e_qt(gq, qt))

        for gq in range(NBG):
            for slot in range(NSC + 1):
                for j in range(KCS):
                    if slot < NSC:
                        sc = slot
                        scg = gq * NSC + sc
                        kc = j
                        kcg = kcg_of(gq, sc, kc)
                        if kcg == 0 and bsub >= 5:
                            vBO.wait(te, 1)
                        if kc == 0 and sc == 0 and gq > 0 and bsub >= 4:
                            vTRC.wait(te, ttr_b(gq - 1, QT - 1, 1))
                        hsc = 2 * scg + (kc // 4)
                        if kc % 4 == 0:
                            sKT[kc // 4].wait(te, DC * (scg - gq + 1))
                        if kc == 0 and sc == 0:
                            sQT.wait(te, DC * gq + DC)
                        if kcg > 0:
                            sTH.wait(te, 2 * kcg)
                            vTH.wait(te, 2 * kcg - 1)
                        last = None
                        for ec in range(DC):
                            for h2 in range(2):
                                kof = ec * 512 + (kc % 4) * 128
                                last = te.matmul(
                                    ps_s[:, h2 * 512:h2 * 512 + 512],
                                    kT_sb[hsc % 2][:, kof:kof + 128],
                                    qT_sb[:, ec * BQ + h2 * 512:ec * BQ + h2 * 512 + 512],
                                    start=(ec == 0), stop=(ec == DC - 1))
                        tSMM.inc(last, kcg + 1)
                    if slot > 0 and bsub >= 2:
                        sc = slot - 1
                        scg = gq * NSC + sc
                        qt = j
                        psl = (scg % 2) * KCS * BQ
                        for kc in range(KCS):
                            kcg = kcg_of(gq, sc, kc)
                            if qt == 0:
                                sPT.wait(te, kcg + 1)
                            lv = scg - gq
                            if kc == 0:
                                if qt == 0:
                                    sV[lv % 2].wait(te, lv // 2 + 1)
                                    vLA.wait(te, scg)
                                vOE.wait(te, QT * scg + qt - 1)
                                if slot == NSC and qt >= 2 and bsub >= 5:
                                    vYE.wait(te, e_qt(gq, qt) - 2)
                            pts = PT[:, psl + kc * BQ + qt * 128:psl + kc * BQ + qt * 128 + 128]
                            te.matmul(ps_o[qt % 2][:, 0:512], pts,
                                      v_sb[lv % 2][:, kc * DM:kc * DM + 512],
                                      start=(kc == 0), stop=(kc == KCS - 1),
                                      skip_group_check=True)
                            te.matmul(ps_o[qt % 2][:, 512:1024], pts,
                                      v_sb[lv % 2][:, kc * DM + 512:kc * DM + 1024],
                                      start=(kc == 0), stop=(kc == KCS - 1),
                                      skip_group_check=True)
                            last = te.matmul(ps_l[:, qt:qt + 1], pts, ones[:, 0:1],
                                             start=(kc == 0), stop=(kc == KCS - 1),
                                             skip_group_check=True)
                            if kc == KCS - 1:
                                tPV.inc(last, tpv(scg, qt))
                        if slot == NSC and qt >= 1 and bsub >= 4:
                            _epi_pe(gq, qt - 1)
            if bsub < 4:
                continue
            _epi_pe(gq, QT - 1)
        if phases < 3:
            return
        for gc in range(NCG):
            vNRM.wait(te, vnrm_c(gc, CT - 1))
            for ec in range(DC):
                if TTR_B + DC * gc + ec >= TTR_B + 2:
                    vTRC.wait(te, ttr_c(gc, ec) - 2)
                trb = [ps_tr0, c_tr1][ec % 2]
                last = None
                for i in range(CT):
                    last = te.transpose(
                        trb[:, i * 128:i * 128 + 128],
                        x2ln[:, i * DM + ec * 128:i * DM + ec * 128 + 128], ident[:])
                tTR.inc(last, ttr_c(gc, ec))
            vTRC.wait(te, ttr_c(gc, DC - 1))
            if gc == 0:
                sCW.wait(te, DC // 2)
                sCW3.wait(te, DC // 2)
            for hcs in range(NHS):
                for hc in range(hcs * HSUP, (hcs + 1) * HSUP):
                    ghc = HC * gc + hc
                    if ghc >= 2:
                        sSIL.wait(te, ghc - 1)
                    last = None
                    for dc in range(DC):
                        last = te.matmul(ps_g[hc % 2][:],
                                         wg_sb[:, dc * HID + hc * 128:dc * HID + hc * 128 + 128],
                                         x2lnT[:, dc * 512:(dc + 1) * 512],
                                         start=(dc == 0), stop=(dc == DC - 1))
                    if ghc == 0:
                        sCW2.wait(te, DC)
                    if ghc >= 2:
                        vHT.wait(te, ghc - 1)
                    for dc in range(DC):
                        last = te.matmul(ps_u[hc % 2][:],
                                         wu_sb[:, dc * HID + hc * 128:dc * HID + hc * 128 + 128],
                                         x2lnT[:, dc * 512:(dc + 1) * 512],
                                         start=(dc == 0), stop=(dc == DC - 1))
                    tGUP.inc(last, cgup(gc, hc))
                ghcs_ = gc * NHS + hcs
                sWD[ghcs_ % 2].wait(te, ghcs_ // 2 + 1)
                vHT.wait(te, HC * gc + hcs * HSUP + HSUP)
                for qt in range(CT):
                    for half in range(2):
                        H = cdmm(gc, hcs, qt, half) - 1
                        if H >= 2:
                            vY2A.wait(te, H - 1)
                        last = None
                        for j in range(HSUP):
                            hc = hcs * HSUP + j
                            hsl = (hc % (2 * HSUP)) * 512
                            last = te.matmul(
                                ps_y2[:, half * 512:half * 512 + 512],
                                hT[:, hsl + qt * 128:hsl + qt * 128 + 128],
                                wdsup[hcs % 2][:, j * DM + half * 512:j * DM + half * 512 + 512],
                                start=(j == 0), stop=(j == HSUP - 1))
                        tDMM.inc(last, cdmm(gc, hcs, qt, half))

    nc._cfg = dict(SEQ=SEQ, DM=DM, HID=HID, OWN=OWN)
    return nc


def _prep_inputs(L0_attn_x, L0_attn_Wq, L0_attn_bq, L0_attn_Wk, L0_attn_bk,
                 L0_attn_Wv, L0_attn_bv, L0_attn_Wo, L0_attn_bo,
                 L0_ffn_w_gate, L0_ffn_w_up, L0_ffn_w_down):
    DM = L0_attn_Wq.shape[0]
    DC = DM // 128
    x = np.ascontiguousarray(np.asarray(L0_attn_x, dtype=np.float32))
    shared = {
        "wk_t": np.ascontiguousarray(np.asarray(L0_attn_Wk, np.float32).T),
        "wv_t": np.ascontiguousarray(np.asarray(L0_attn_Wv, np.float32).T),
        "wq_t": np.ascontiguousarray(np.asarray(L0_attn_Wq, np.float32).T),
        "wo_t": np.ascontiguousarray(np.asarray(L0_attn_Wo, np.float32).T),
        "bq_c": np.ascontiguousarray(np.asarray(L0_attn_bq, np.float32).reshape(DC, 128).T),
        "bk_c": np.ascontiguousarray(np.asarray(L0_attn_bk, np.float32).reshape(DC, 128).T),
        "bv_c": np.ascontiguousarray(np.asarray(L0_attn_bv, np.float32).reshape(DC, 128).T),
        "bo_r": np.ascontiguousarray(np.asarray(L0_attn_bo, np.float32).reshape(1, DM)),
        "wg_i": np.ascontiguousarray(np.asarray(L0_ffn_w_gate, np.int32).T),
        "wu_i": np.ascontiguousarray(np.asarray(L0_ffn_w_up, np.int32).T),
        "wd_i": np.ascontiguousarray(np.asarray(L0_ffn_w_down, np.int32).T),
        "ident_h": np.eye(128, dtype=BFNP),
        "ones_h": np.ones((128, 512), dtype=BFNP),
        "consts_h": np.stack([np.full(128, 0.6 - C_OFF, np.float32),
                              np.full(128, EPS, np.float32),
                              np.zeros(128, np.float32),
                              np.zeros(128, np.float32)], axis=1),
    }
    return x, shared


def kernel(L0_attn_x, L0_attn_Wq, L0_attn_bq, L0_attn_Wk, L0_attn_bk, L0_attn_Wv,
           L0_attn_bv, L0_attn_Wo, L0_attn_bo, L0_ffn_w_gate, L0_ffn_w_up,
           L0_ffn_w_down):
    x, shared = _prep_inputs(L0_attn_x, L0_attn_Wq, L0_attn_bq, L0_attn_Wk,
                             L0_attn_bk, L0_attn_Wv, L0_attn_bv, L0_attn_Wo,
                             L0_attn_bo, L0_ffn_w_gate, L0_ffn_w_up, L0_ffn_w_down)
    B, S, DM = x.shape
    OWN = S // 2
    nc = build_program(SEQ=S, DM=DM, HID=L0_ffn_w_gate.shape[0])
    in_maps = []
    for core in range(8):
        b, h = core // 2, core % 2
        m = dict(shared)
        m["x_own"] = np.ascontiguousarray(x[b, h * OWN:(h + 1) * OWN])
        m["x_oth"] = np.ascontiguousarray(x[b, (1 - h) * OWN:(2 - h) * OWN])
        in_maps.append(m)
    res = run_bass_kernel_spmd(nc, in_maps, core_ids=list(range(8)))
    out = np.empty((B, S, DM), dtype=np.float32)
    for core in range(8):
        b, h = core // 2, core % 2
        out[b, h * OWN:(h + 1) * OWN] = res.results[core]["y_out"]
    return out


# version 21
# speedup vs baseline: 1.0331x; 1.0331x over previous
import contextlib

import numpy as np
import ml_dtypes

import concourse.bass as bass
import concourse.mybir as mybir
from concourse.bass_utils import run_bass_kernel_spmd

F32 = mybir.dt.float32
BF16 = mybir.dt.bfloat16
I32 = mybir.dt.int32
AF = mybir.ActivationFunctionType
ALU = mybir.AluOpType
BFNP = ml_dtypes.bfloat16

C_OFF = 30.0
EPS = 1e-5
DEQ = float(np.float32(1.0 / 127.0))


class Tok:

    def __init__(self, handle, step=1):
        self.h = handle
        self.step = step
        self.n = 0

    def inc(self, ins, expect=None):
        self.n += self.step
        ins.then_inc(self.h, self.step)
        if expect is not None:
            assert self.n == expect * self.step, (self.n, expect, self.step)

    def wait(self, eng, expect):
        v = int(expect) * self.step
        if v > 0:
            eng.wait_ge(self.h, v)


class SelfSem:

    def __init__(self, handle):
        self.h = handle
        self.n = 0

    def edge(self, eng, ins):
        self.n += 1
        ins.then_inc(self.h, 1)
        eng.wait_ge(self.h, self.n)


def build_program(SEQ=8192, DM=1024, HID=3072, debug=False, phases=3, asub=3, bsub=6):
    assert DM == 1024 and HID % 512 == 0 and SEQ % 2048 == 0
    OWN = SEQ // 2
    DC = DM // 128
    HC = HID // 128
    AG = 512
    NAG = SEQ // AG
    NOG = OWN // AG
    BQ = 1024
    NBG = OWN // BQ
    QT = BQ // 128
    SCn = 1024
    NSC = SEQ // SCn
    KCS = SCn // 128
    CG = 512
    NCG = OWN // CG
    CT = CG // 128
    HSUP = 4
    NHS = HC // HSUP
    NW = 4 * DC
    NI = 2 * DC + HC
    NU = 4 * DC + HC
    HH = HID // 2
    INV_DM = float(np.float32(1.0 / DM))

    nc = bass.Bass("TRN2", target_bir_lowering=False, debug=False)

    x_own = nc.dram_tensor("x_own", [OWN, DM], F32, kind="ExternalInput").ap()
    x_oth = nc.dram_tensor("x_oth", [OWN, DM], F32, kind="ExternalInput").ap()
    wk_t = nc.dram_tensor("wk_t", [DM, DM], F32, kind="ExternalInput").ap()
    wv_t = nc.dram_tensor("wv_t", [DM, DM], F32, kind="ExternalInput").ap()
    wq_t = nc.dram_tensor("wq_t", [DM, DM], F32, kind="ExternalInput").ap()
    wo_t = nc.dram_tensor("wo_t", [DM, DM], F32, kind="ExternalInput").ap()
    bq_c = nc.dram_tensor("bq_c", [128, DC], F32, kind="ExternalInput").ap()
    bk_c = nc.dram_tensor("bk_c", [128, DC], F32, kind="ExternalInput").ap()
    bo_r = nc.dram_tensor("bo_r", [1, DM], F32, kind="ExternalInput").ap()
    bv_c = nc.dram_tensor("bv_c", [128, DC], F32, kind="ExternalInput").ap()
    wg_i = nc.dram_tensor("wg_i", [DM, HID], I32, kind="ExternalInput").ap()
    wu_i = nc.dram_tensor("wu_i", [DM, HID], I32, kind="ExternalInput").ap()
    wd_i = nc.dram_tensor("wd_i", [HID, DM], I32, kind="ExternalInput").ap()
    ident_h = nc.dram_tensor("ident_h", [128, 128], BF16, kind="ExternalInput").ap()
    ones_h = nc.dram_tensor("ones_h", [128, 512], BF16, kind="ExternalInput").ap()
    consts_h = nc.dram_tensor("consts_h", [128, 4], F32, kind="ExternalInput").ap()

    y_out = nc.dram_tensor("y_out", [OWN, DM], F32, kind="ExternalOutput").ap()

    scr = "ExternalOutput" if debug else "Internal"
    kT_scr = nc.dram_tensor("kT_scr", [DC, 128, SEQ], BF16, kind=scr).ap()
    v_scr = nc.dram_tensor("v_scr", [SEQ // 128, 128, DM], BF16, kind=scr).ap()
    qT_scr = nc.dram_tensor("qT_scr", [DC, 128, OWN], BF16, kind=scr).ap()
    x2_scr = nc.dram_tensor("x2_scr", [OWN, DM], BF16, kind=scr).ap()
    wg_scr = nc.dram_tensor("wg_scr", [DC, 128, HID], BF16, kind=scr).ap()
    wu_scr = nc.dram_tensor("wu_scr", [DC, 128, HID], BF16, kind=scr).ap()
    wd_scr = nc.dram_tensor("wd_scr", [HC, 128, DM], BF16, kind=scr).ap()
    bo_scr = nc.dram_tensor("bo_scr", [1, DM], BF16, kind=scr).ap()
    if debug:
        dbg_o = nc.dram_tensor("dbg_o", [128, QT * DM], F32, kind="ExternalOutput").ap()
        dbg_y = nc.dram_tensor("dbg_y", [128, QT * DM], F32, kind="ExternalOutput").ap()
        dbg_l = nc.dram_tensor("dbg_l", [128, QT], F32, kind="ExternalOutput").ap()

    ctx = contextlib.ExitStack()

    gsb = lambda n, s, d: ctx.enter_context(nc.sbuf_tensor(n, s, d))
    ident = gsb("g_ident", [128, 128], BF16)
    ones = gsb("g_ones", [128, 512], BF16)
    consts = gsb("g_consts", [128, 4], F32)
    wo_bf = gsb("g_wo_bf", [128, DC * DM], BF16)
    dump = gsb("g_dump", [128, 1024], F32)
    eTMP = gsb("g_etmp", [128, 1024], F32)
    ps_tr0 = ctx.enter_context(nc.psum_tensor("ps_tr0", [128, 512], BF16))

    cA = contextlib.ExitStack()
    sbA = lambda n, s, d: cA.enter_context(nc.sbuf_tensor(n, s, d))
    ptA = lambda n, s, d: cA.enter_context(nc.psum_tensor(n, s, d))
    bqc = sbA("a_bqc", [128, DC], F32)
    bkc = sbA("a_bkc", [128, DC], F32)
    bor = sbA("a_bor", [1, DM], F32)
    bvc = sbA("a_bvc", [128, DC], F32)
    bvb = sbA("a_bvb", [128, DC], BF16)
    boP = sbA("a_boP", [1, DM], BF16)
    xg = [sbA(f"a_xg{i}", [128, 4 * DM], F32) for i in range(2)]
    xln = [sbA(f"a_xln{i}", [128, 4 * DM], BF16) for i in range(2)]
    xlnT = [sbA(f"a_xlnT{i}", [128, DC * 512], BF16) for i in range(2)]
    wk_bf = sbA("a_wk_bf", [128, DC * DM], BF16)
    wv_bf = sbA("a_wv_bf", [128, DC * DM], BF16)
    wq_bf = sbA("a_wq_bf", [128, DC * DM], BF16)
    wstage = [sbA(f"a_wstage{i}", [128, DM], F32) for i in range(2)]
    istage = [sbA(f"a_istage{i}", [128, HID // 2], I32) for i in range(4)]
    bstage = [sbA(f"a_bstage{i}", [128, HID // 2], BF16) for i in range(5)]
    kstage = [sbA(f"a_kstage{i}", [128, 512], BF16) for i in range(4)]
    qstage = [sbA(f"a_qstage{i}", [128, 512], BF16) for i in range(4)]
    vstage = [sbA(f"a_vstage{i}", [128, DM], BF16) for i in range(4)]
    amu = sbA("a_amu", [128, 8], F32)
    anmr = sbA("a_anmr", [128, 8], F32)
    arstd = sbA("a_arstd", [128, 8], F32)
    asum = sbA("a_asum", [128, 8], F32)
    asq = sbA("a_asq", [128, 8], F32)
    avar = sbA("a_avar", [128, 4], F32)
    alnv = sbA("a_alnv", [128, 4], F32)
    alnv2 = sbA("a_alnv2", [128, 4], F32)
    a_tr1 = ptA("a_tr1", [128, 512], BF16)
    ps_k = [ptA(f"a_ps_k{i}", [128, 512], F32) for i in range(2)]
    ps_v = [ptA(f"a_ps_v{i}", [128, 512], F32) for i in range(2)]
    ps_q = [ptA(f"a_ps_q{i}", [128, 512], F32) for i in range(2)]
    cA.close()

    cB = contextlib.ExitStack()
    sbB = lambda n, s, d: cB.enter_context(nc.sbuf_tensor(n, s, d))
    ptB = lambda n, s, d: cB.enter_context(nc.psum_tensor(n, s, d))
    boB = sbB("b_boB", [128, DM], BF16)
    qT_sb = sbB("b_qT", [128, DC * BQ], BF16)
    kT_sb = [sbB(f"b_kT{i}", [128, DC * 512], BF16) for i in range(2)]
    v_sb = [sbB(f"b_v{i}", [128, KCS * DM], BF16) for i in range(2)]
    o_acc = sbB("b_oacc", [128, QT * DM], F32)
    PT = sbB("b_PT", [128, 2 * KCS * BQ], BF16)
    ta = sbB("b_ta", [128, 2 * BQ], F32)
    tb = sbB("b_tb", [128, 2 * BQ], F32)
    onb = sbB("b_onb", [128, DM], BF16)
    onT = sbB("b_onT", [128, DM], BF16)
    y_sb = sbB("b_y", [128, QT * DM], BF16)
    xo_sb = sbB("b_xo", [128, DM], F32)
    x2st = sbB("b_x2st", [128, DM], BF16)
    linv = sbB("b_linv", [128, QT], F32)
    l_acc = sbB("b_lacc", [128, QT], F32)
    ysum = sbB("b_ysum", [128, QT], F32)
    yss = sbB("b_yss", [128, QT], F32)
    ymu = sbB("b_ymu", [128, QT], F32)
    yvar = sbB("b_yvar", [128, QT], F32)
    ylnv = sbB("b_ylnv", [128, QT], F32)
    ylnv2 = sbB("b_ylnv2", [128, QT], F32)
    yrstd = sbB("b_yrstd", [128, QT], F32)
    ynmr = sbB("b_ynmr", [128, QT], F32)
    ps_s = ptB("b_ps_s", [128, BQ], F32)
    ps_o = [ptB(f"b_ps_o{i}", [128, DM], F32) for i in range(2)]
    ps_l = ptB("b_ps_l", [128, QT], F32)
    cB.close()

    cC = contextlib.ExitStack()
    sbC = lambda n, s, d: cC.enter_context(nc.sbuf_tensor(n, s, d))
    ptC = lambda n, s, d: cC.enter_context(nc.psum_tensor(n, s, d))
    wg_sb = sbC("c_wg", [128, DC * HID], BF16)
    wu_sb = sbC("c_wu", [128, DC * HID], BF16)
    wdsup = [sbC(f"c_wd{i}", [128, HSUP * DM], BF16) for i in range(2)]
    x2g = [sbC(f"c_x2g{i}", [128, CT * DM], BF16) for i in range(2)]
    x2ln = sbC("c_x2ln", [128, CT * DM], BF16)
    x2lnT = sbC("c_x2lnT", [128, DC * CG], BF16)
    hT = sbC("c_hT", [128, 2 * HSUP * 512], BF16)
    g_sb = [sbC(f"c_g{i}", [128, 512], F32) for i in range(2)]
    y2acc = sbC("c_y2acc", [128, CT * DM], F32)
    outst = [sbC(f"c_outst{i}", [128, DM], F32) for i in range(2)]
    cmu = sbC("c_cmu", [128, 8], F32)
    cnmr = sbC("c_cnmr", [128, 8], F32)
    crstd = sbC("c_crstd", [128, 8], F32)
    csum = sbC("c_csum", [128, 8], F32)
    csq = sbC("c_csq", [128, 8], F32)
    cvar = sbC("c_cvar", [128, 4], F32)
    clnv = sbC("c_clnv", [128, 4], F32)
    clnv2 = sbC("c_clnv2", [128, 4], F32)
    y2sum = sbC("c_y2sum", [128, CT], F32)
    y2ss = sbC("c_y2ss", [128, CT], F32)
    y2mu = sbC("c_y2mu", [128, CT], F32)
    y2var = sbC("c_y2var", [128, CT], F32)
    y2lnv = sbC("c_y2lnv", [128, CT], F32)
    y2lnv2 = sbC("c_y2lnv2", [128, CT], F32)
    y2rstd = sbC("c_y2rstd", [128, CT], F32)
    y2nmr = sbC("c_y2nmr", [128, CT], F32)
    c_tr1 = ptC("c_tr1", [128, 512], BF16)
    ps_g = [ptC(f"c_ps_g{i}", [128, 512], F32) for i in range(2)]
    ps_u = [ptC(f"c_ps_u{i}", [128, 512], F32) for i in range(2)]
    ps_y2 = ptC("c_ps_y2", [128, DM], F32)
    cC.close()

    mks = lambda n, step=1: Tok(ctx.enter_context(nc.semaphore(n)), step)
    sPRE = mks("sPRE", 16)
    sWL = [mks(f"sWL{i}", 16) for i in range(2)]
    sIL = [mks(f"sIL{i}", 16) for i in range(4)]
    sXA = [mks(f"sXA{i}", 16) for i in range(2)]
    sKT = [mks(f"sKT{i}", 16) for i in range(2)]
    sV = [mks(f"sV{i}", 16) for i in range(2)]
    sQT = mks("sQT", 16)
    sXO = mks("sXO", 16)
    sBOB = mks("sBOB", 16)
    sCW = mks("sCW", 16)
    sCW2 = mks("sCW2", 16)
    sCX2 = mks("sCX2", 16)
    sWD = [mks(f"sWD{i}", 16) for i in range(2)]
    vWC = mks("vWC")
    vLN = mks("vLN")
    vNRM = mks("vNRM")
    vTRC = mks("vTRC")
    vEV = mks("vEV")
    vTH = mks("vTH")
    vOE = mks("vOE")
    vLR = mks("vLR")
    vLA = mks("vLA")
    vON = mks("vON")
    vYE = mks("vYE")
    vX2 = mks("vX2")
    vHT = mks("vHT")
    vY2A = mks("vY2A")
    vBO = mks("vBO")
    vSELF = SelfSem(ctx.enter_context(nc.semaphore("vSELF")))
    sST = mks("sST")
    sRS = mks("sRS")
    sKQ = mks("sKQ")
    sTH = mks("sTH")
    sPT = mks("sPT")
    sSIL = mks("sSIL")
    sSELF = SelfSem(ctx.enter_context(nc.semaphore("sSELF")))
    tTR = mks("tTR")
    tPRJ = mks("tPRJ")
    tBO = mks("tBO")
    tSMM = mks("tSMM")
    tPV = mks("tPV")
    tYMM = mks("tYMM")
    tGUP = mks("tGUP")
    tDMM = mks("tDMM")
    gDQ = mks("gDQ")
    gDS = [mks(f"gDS{i}", 16) for i in range(2)]
    gKS = [mks(f"gKS{i}", 16) for i in range(4)]
    gQS = [mks(f"gQS{i}", 16) for i in range(4)]
    gVS = [mks(f"gVS{i}", 16) for i in range(4)]
    gBO = mks("gBO", 16)
    gX2 = mks("gX2", 16)
    gOS = [mks(f"gOS{i}", 16) for i in range(2)]

    MISC_N = 7
    pre_w = lambda i: MISC_N + i + 1
    pre_i = lambda j: MISC_N + NW + j + 1
    PRJ_OWN, PRJ_OTH = DC + 8 + DC, DC + 8

    def prj_base(g):
        return PRJ_OWN * min(g, NOG) + PRJ_OTH * max(0, g - NOG)

    prj_k = lambda g, ec: prj_base(g) + ec + 1
    prj_v = lambda g, vj: prj_base(g) + DC + vj + 1
    prj_q = lambda g, ec: prj_base(g) + DC + 8 + ec + 1
    PRJ_END = prj_base(NAG)

    def kq_base(g):
        return 2 * DC * min(g, NOG) + DC * max(0, g - NOG)

    kq_k = lambda g, ec: kq_base(g) + ec + 1
    kq_q = lambda g, ec: kq_base(g) + DC + ec + 1
    KQ_END = kq_base(NAG)
    kq_k_glob = lambda K: kq_k(K // DC, K % DC)
    kq_q_glob = lambda K: kq_q(K // DC, K % DC)
    vev = lambda g, vj: 8 * g + vj + 1

    ttr_a = lambda g, ec: DC * g + ec + 1
    TTR_A = DC * NAG
    ttr_b = lambda gq, qt, w: TTR_A + 2 * (QT * gq + qt) + w + 1
    TTR_B = TTR_A + 2 * QT * NBG
    ttr_c = lambda gc, ec: TTR_B + DC * gc + ec + 1
    vnrm_a = lambda g, i: 4 * g + i + 1
    VNRM_A = 4 * NAG
    vnrm_c = lambda gc, i: VNRM_A + 4 * gc + i + 1
    sst_a = lambda g: 8 * g + 8
    SST_A = 8 * NAG
    sst_b = lambda gq, qt: SST_A + QT * gq + qt + 1
    SST_B = SST_A + QT * NBG
    sst_cx = lambda gc: SST_B + (8 if gc == 0 else 16 * gc)
    sst_cy = lambda gc, ct: (SST_B + 16 * gc
                             + (16 if gc + 1 < NCG else 8) + 2 * (ct + 1))
    vln_a = lambda g: g + 1
    VLN_A = NAG
    vln_b = lambda gq: VLN_A + gq + 1
    VLN_B = VLN_A + NBG
    vln_cx = lambda gc: VLN_B + (1 if gc == 0 else 2 * gc)
    vln_cy = lambda gc: VLN_B + 2 * gc + (3 if gc + 1 < NCG else 2)
    srs_a = lambda g: g + 1
    SRS_A = NAG
    srs_b = lambda gq: SRS_A + gq + 1
    SRS_B = SRS_A + NBG
    srs_cx = lambda gc: SRS_B + (1 if gc == 0 else 2 * gc)
    srs_cy = lambda gc: SRS_B + 2 * gc + (3 if gc + 1 < NCG else 2)
    kcg_of = lambda gq, sc, kc: (gq * NSC + sc) * KCS + kc
    tpv = lambda scg, qt: QT * scg + qt + 1
    e_qt = lambda gq, qt: QT * gq + qt + 1
    VX2_B = QT * NBG
    vx2_c = lambda gc, ct: VX2_B + CT * gc + ct + 1
    cgup = lambda gc, hc: HC * gc + hc + 1
    csil = lambda gc, hc: HC * gc + hc + 1
    cht = lambda gc, hc: HC * gc + hc + 1
    cdmm = lambda gc, hcs, qt, half: ((gc * NHS + hcs) * CT + qt) * 2 + half + 1

    def a_xsrc(g):
        if g < NOG:
            return x_own[g * AG:(g + 1) * AG, :]
        return x_oth[(g - NOG) * AG:(g + 1 - NOG) * AG, :]

    def dq_parts(u):
        if u < 2 * DC:
            j, h = u // 2, u % 2
            return (wg_i[j * 128:(j + 1) * 128, h * HH:(h + 1) * HH],
                    wg_scr[j][:, h * HH:(h + 1) * HH], HH)
        if u < 4 * DC:
            uu = u - 2 * DC
            j, h = uu // 2, uu % 2
            return (wu_i[j * 128:(j + 1) * 128, h * HH:(h + 1) * HH],
                    wu_scr[j][:, h * HH:(h + 1) * HH], HH)
        j = u - 4 * DC
        return wd_i[j * 128:(j + 1) * 128, :], wd_scr[j], DM

    blk = ctx.enter_context(nc.Block())

    @blk.sync
    def _(sync):
        def dma(out, in_, tok, expect=None):
            ins = sync.dma_start(out=out, in_=in_)
            tok.inc(ins, expect)

        def xload(g):
            sXA[g % 2].wait(sync, g // 2)
            vNRM.wait(sync, 4 * (g - 2) + 4 if g >= 2 else 0)
            dma(xg[g % 2][:].rearrange("p (i d) -> p i d", i=4),
                a_xsrc(g).rearrange("(i p) d -> p i d", p=128), sXA[g % 2], g // 2 + 1)

        xload(0)
        xload(1)
        dma(ident[:], ident_h, sPRE, 1)
        dma(ones[:], ones_h, sPRE, 2)
        dma(consts[:], consts_h, sPRE, 3)
        dma(bqc[:], bq_c, sPRE, 4)
        dma(bkc[:], bk_c, sPRE, 5)
        dma(bor[:], bo_r, sPRE, 6)
        dma(bvc[:], bv_c, sPRE, 7)
        wsrc = [wk_t, wv_t, wq_t, wo_t]
        sint = [0]

        def int_load():
            u = sint[0]
            if u >= NU:
                return
            sint[0] += 1
            src_, _, w = dq_parts(u)
            sIL[u % 4].wait(sync, u // 4)
            gDQ.wait(sync, u - 3)
            sIL[u % 4].inc(sync.dma_start(out=istage[u % 4][:, 0:w], in_=src_),
                           u // 4 + 1)

        for i in range(NW):
            sWL[i % 2].wait(sync, i // 2)
            vWC.wait(sync, i - 1)
            dma(wstage[i % 2][:], wsrc[i // DC][(i % DC) * 128:(i % DC + 1) * 128, :],
                sWL[i % 2], i // 2 + 1)
            if i in (23, 27, 29, 30, 31):
                int_load()

        for g in range(2, NAG):
            xload(g)
            for _t in range(5):
                int_load()
        while sint[0] < NU:
            int_load()

        if phases < 2:
            return
        for p in range(4):
            gKS[p].wait(sync, DC * NAG // 4)
            gVS[p].wait(sync, 4 * NAG // 4)
            gQS[p].wait(sync, DC * NOG // 4)
        for p in range(2):
            gDS[p].wait(sync, NU // 2)
        if bsub >= 5:
            gBO.wait(sync, 1)
            dma(boB[:], bo_scr.partition_broadcast(128), sBOB, 1)
        v_reader = {}
        for gq in range(NBG):
            sQT.wait(sync, DC * gq)
            tSMM.wait(sync, KCS * NSC * gq)
            for ec in range(DC):
                Kq = DC * (2 * gq + 1) + ec
                gQS[Kq % 4].wait(sync, Kq // 4 + 1)
                dma(qT_sb[:, ec * BQ:(ec + 1) * BQ],
                    qT_scr[ec, :, gq * BQ:(gq + 1) * BQ], sQT, DC * gq + ec + 1)
            def xo_tiles(sc_i):
                for qtx in range(sc_i * QT // NSC, (sc_i + 1) * QT // NSC):
                    t = QT * (gq - 1) + qtx
                    sXO.wait(sync, t)
                    vX2.wait(sync, t)
                    dma(xo_sb[:], x_own[t * 128:(t + 1) * 128, :], sXO, t + 1)

            for sc in range(NSC):
                scg = gq * NSC + sc
                if gq >= 1 and sc == 0:
                    if bsub >= 6:
                        xo_tiles(0)
                    continue
                sd = sc if gq % 2 == 0 else NSC - 1 - sc
                lidx = scg - gq
                for h in range(2):
                    hsc = 2 * scg + h
                    sKT[h].wait(sync, DC * lidx)
                    tSMM.wait(sync, 4 * (hsc - 1))
                    for ec in range(DC):
                        if gq == 0:
                            Kk = DC * (2 * sd + h) + ec
                            gKS[Kk % 4].wait(sync, Kk // 4 + 1)
                        dma(kT_sb[h][:, ec * 512:(ec + 1) * 512],
                            kT_scr[ec, :, sd * SCn + h * 512:sd * SCn + h * 512 + 512],
                            sKT[h], DC * lidx + ec + 1)
                sV[lidx % 2].wait(sync, lidx // 2)
                if lidx % 2 in v_reader:
                    tPV.wait(sync, QT * (v_reader[lidx % 2] + 1))
                if gq == 0:
                    Vv = 4 * (2 * sd + 1) + 3
                    gVS[Vv % 4].wait(sync, Vv // 4 + 1)
                dma(v_sb[lidx % 2][:].rearrange("p (c e) -> p c e", c=KCS),
                    v_scr[sd * KCS:(sd + 1) * KCS].rearrange("c p e -> p c e"),
                    sV[lidx % 2], lidx // 2 + 1)
                nxt = scg + 1
                v_reader[lidx % 2] = scg + (1 if (nxt % NSC == 0 and
                                                  nxt < NSC * NBG) else 0)
                if bsub >= 6 and gq >= 1:
                    xo_tiles(sc)
        if bsub >= 6:
            for qt in range(QT):
                t = QT * (NBG - 1) + qt
                sXO.wait(sync, t)
                vX2.wait(sync, t)
                dma(xo_sb[:], x_own[t * 128:(t + 1) * 128, :], sXO, t + 1)

        if phases < 3:
            return
        tYMM.wait(sync, QT * NBG)
        vX2.wait(sync, VX2_B)
        sST.wait(sync, SST_B)
        gX2.wait(sync, QT * NBG)
        gX2.wait(sync, CT)
        dma(x2g[0][:].rearrange("p (i d) -> p i d", i=CT),
            x2_scr[0:CG, :].rearrange("(i p) d -> p i d", p=128), sCX2, 1)
        for dc in range(DC):
            for p in range(2):
                gDS[p].wait(sync, dc + 1)
            dma(wg_sb[:, dc * HID:(dc + 1) * HID], wg_scr[dc], sCW, dc + 1)
        sCW.wait(sync, DC)
        for hcs in range(NHS):
            sWD[hcs % 2].wait(sync, hcs // 2)
            if hcs >= 2:
                tDMM.wait(sync, (hcs - 1) * CT * 2)
            jmax = 4 * DC + hcs * HSUP + HSUP - 1
            gDS[jmax % 2].wait(sync, jmax // 2 + 1)
            gDS[(jmax - 1) % 2].wait(sync, (jmax - 1) // 2 + 1)
            dma(wdsup[hcs % 2][:].rearrange("p (h d) -> p h d", h=HSUP),
                wd_scr[hcs * HSUP:(hcs + 1) * HSUP].rearrange("h p d -> p h d"),
                sWD[hcs % 2], hcs // 2 + 1)
        for gc in range(1, NCG):
            sCX2.wait(sync, gc - 1)
            gX2.wait(sync, CT * gc + CT)
            if gc >= 2:
                vX2.wait(sync, vx2_c(gc - 2, CT - 1))
            dma(x2g[gc % 2][:].rearrange("p (i d) -> p i d", i=CT),
                x2_scr[gc * CG:(gc + 1) * CG, :].rearrange("(i p) d -> p i d", p=128),
                sCX2, gc + 1)
            for hcs in range(NHS):
                ghcs = gc * NHS + hcs
                sWD[ghcs % 2].wait(sync, ghcs // 2)
                if ghcs >= 2:
                    tDMM.wait(sync, (ghcs - 1) * CT * 2)
                jmax = 4 * DC + hcs * HSUP + HSUP - 1
                gDS[jmax % 2].wait(sync, jmax // 2 + 1)
                gDS[(jmax - 1) % 2].wait(sync, (jmax - 1) // 2 + 1)
                dma(wdsup[ghcs % 2][:].rearrange("p (h d) -> p h d", h=HSUP),
                    wd_scr[hcs * HSUP:(hcs + 1) * HSUP].rearrange("h p d -> p h d"),
                    sWD[ghcs % 2], ghcs // 2 + 1)

    @blk.gpsimd
    def _(gp):
        for g in range(NAG):
            own = g < NOG
            if asub >= 3:
                for ec in range(DC):
                    K = DC * g + ec
                    gKS[K % 4].wait(gp, K // 4)
                    sKQ.wait(gp, kq_k(g, ec))
                    gKS[K % 4].inc(gp.dma_start(out=kT_scr[ec, :, g * AG:(g + 1) * AG],
                                                in_=kstage[ec % 4][:]), K // 4 + 1)
            if asub >= 3:
                for i in range(4):
                    V = 4 * g + i
                    gVS[V % 4].wait(gp, V // 4)
                    vEV.wait(gp, vev(g, 2 * i + 1))
                    gVS[V % 4].inc(gp.dma_start(out=v_scr[4 * g + i],
                                                in_=vstage[i % 4][:]), V // 4 + 1)
                if own:
                    for ec in range(DC):
                        K = DC * g + ec
                        gQS[K % 4].wait(gp, K // 4)
                        sKQ.wait(gp, kq_q(g, ec))
                        gQS[K % 4].inc(gp.dma_start(out=qT_scr[ec, :, g * AG:(g + 1) * AG],
                                                    in_=qstage[ec % 4][:]), K // 4 + 1)
        if phases < 2:
            for p in range(2):
                gp.wait_ge(gDS[p].h, 16 * NU // 2)
            if asub == 1:
                gp.wait_ge(vNRM.h, 4 * NAG)
            elif asub == 2:
                gp.wait_ge(vTRC.h, DC * NAG)
            else:
                for p in range(4):
                    gp.wait_ge(gKS[p].h, 16 * DC * NAG // 4)
                    gp.wait_ge(gVS[p].h, 16 * 4 * NAG // 4)
                    gp.wait_ge(gQS[p].h, 16 * DC * NOG // 4)
            return
        if bsub < 6:
            if bsub == 1:
                gp.wait_ge(sPT.h, KCS * NSC * NBG)
            elif bsub == 2:
                gp.wait_ge(vOE.h, QT * NSC * NBG)
                gp.wait_ge(vLA.h, NSC * NBG)
            elif bsub == 3:
                import os as _os
                gp.wait_ge(vON.h, int(_os.environ.get("BQT", QT)) * NBG)
            elif bsub == 4:
                gp.wait_ge(vTRC.h, TTR_A + 2 * QT * NBG)
            else:
                gp.wait_ge(sST.h, SST_B)
            if bsub == 5:
                vBO.wait(gp, 1)
                gBO.inc(gp.dma_start(out=bo_scr, in_=boP[:]), 1)
            return
        vBO.wait(gp, 1)
        gBO.inc(gp.dma_start(out=bo_scr, in_=boP[:]), 1)
        for gq in range(NBG):
            for qt in range(QT):
                t = QT * gq + qt
                gX2.wait(gp, t)
                vX2.wait(gp, t + 1)
                gX2.inc(gp.dma_start(out=x2_scr[t * 128:(t + 1) * 128, :],
                                     in_=x2st[:]), t + 1)
        if phases < 3:
            gp.wait_ge(gX2.h, 16 * QT * NBG)
            if debug:
                gp.dma_start(out=dbg_o, in_=o_acc[:]).then_inc(gOS.h, 16)
                gp.dma_start(out=dbg_y, in_=y_sb[:]).then_inc(gOS.h, 16)
                gp.dma_start(out=dbg_l, in_=linv[:]).then_inc(gOS.h, 16)
                gp.wait_ge(gOS.h, 48)
            return
        for gc in range(NCG):
            for ct in range(CT):
                t = CT * gc + ct
                gOS[t % 2].wait(gp, t // 2)
                vX2.wait(gp, VX2_B + t + 1)
                gOS[t % 2].inc(gp.dma_start(out=y_out[t * 128:(t + 1) * 128, :],
                                            in_=outst[t % 2][:]), t // 2 + 1)
        for p in range(2):
            gp.wait_ge(gOS[p].h, 16 * CT * NCG // 2)

    @blk.scalar
    def _(sc_):
        def S(ins, tok=None, expect=None):
            if tok is None:
                sSELF.n += 1
                ins.then_inc(sSELF.h, 1)
                sc_.wait_ge(sSELF.h, sSELF.n)
            else:
                tok.inc(ins, expect)
                sc_.wait_ge(tok.h, tok.n)
            return ins

        sPRE.wait(sc_, MISC_N)
        sstore = [0]

        def bf_store():
            u = sstore[0]
            if u >= NU:
                return
            sstore[0] += 1
            _, dst, w = dq_parts(u)
            gDS[u % 2].wait(sc_, u // 2)
            gDQ.wait(sc_, u + 1)
            gDS[u % 2].inc(sc_.dma_start(out=dst, in_=bstage[u % 5][:, 0:w]),
                           u // 2 + 1)

        def a_stats(g):
            p4 = (g % 2) * 4
            sXA[g % 2].wait(sc_, g // 2 + 1)
            for i in range(4):
                S(sc_.activation(dump[:], xg[g % 2][:, i * DM:(i + 1) * DM], AF.Copy,
                                 accum_out=asum[:, p4 + i:p4 + i + 1]), sST, 8 * g + i + 1)
            for i in range(4):
                S(sc_.activation(dump[:], xg[g % 2][:, i * DM:(i + 1) * DM], AF.Square,
                                 accum_out=asq[:, p4 + i:p4 + i + 1]), sST, 8 * g + 4 + i + 1)

        def a_rstd(g):
            p4 = (g % 2) * 4
            vLN.wait(sc_, vln_a(g))
            S(sc_.activation(alnv[:], avar[:], AF.Ln, bias=consts[:, 1:2]))
            S(sc_.activation(arstd[:, p4:p4 + 4], alnv[:], AF.Exp, scale=-0.5),
              sRS, srs_a(g))

        a_stats(0)
        a_rstd(0)
        for g in range(NAG):
            own = g < NOG
            if asub >= 3:
                for ec in range(DC):
                    tPRJ.wait(sc_, prj_k(g, ec))
                    K = DC * g + ec
                    gKS[K % 4].wait(sc_, K // 4)
                    S(sc_.activation(kstage[ec % 4][:], ps_k[ec % 2][:], AF.Identity,
                                     bias=bkc[:, ec:ec + 1]), sKQ, kq_k(g, ec))
                if g + 1 < NAG:
                    a_stats(g + 1)
                if own:
                    for ec in range(DC):
                        tPRJ.wait(sc_, prj_q(g, ec))
                        K = DC * g + ec
                        gQS[K % 4].wait(sc_, K // 4)
                        S(sc_.activation(qstage[ec % 4][:], ps_q[ec % 2][:], AF.Identity,
                                         bias=bqc[:, ec:ec + 1]), sKQ, kq_q(g, ec))
                if g + 1 < NAG:
                    a_rstd(g + 1)
                if g >= 1:
                    for _t in range(5):
                        bf_store()
            elif g + 1 < NAG:
                a_stats(g + 1)
                a_rstd(g + 1)
        while sstore[0] < NU:
            bf_store()
        if phases < 2:
            return
        for gq in range(NBG):
            for sc in range(NSC):
                scg = gq * NSC + sc
                for kc in range(KCS):
                    kcg = kcg_of(gq, sc, kc)
                    hh = (kcg % 2) * BQ
                    tSMM.wait(sc_, kcg + 1)
                    vTH.wait(sc_, 2 * kcg - 2)
                    S(sc_.activation(ta[:, hh:hh + BQ], ps_s[:], AF.Tanh, scale=0.5),
                      sTH, 2 * kcg + 1)
                    S(sc_.activation(tb[:, hh:hh + BQ], ps_s[:], AF.Tanh),
                      sTH, 2 * kcg + 2)
                    vTH.wait(sc_, 2 * kcg + 2)
                    if kc == 0 and bsub >= 2:
                        tPV.wait(sc_, QT * (scg - 1))
                    psl = (scg % 2) * KCS * BQ
                    S(sc_.activation(PT[:, psl + kc * BQ:psl + (kc + 1) * BQ],
                                     tb[:, hh:hh + BQ],
                                     AF.Exp, scale=0.1, bias=consts[:, 0:1]),
                      sPT, kcg + 1)
            if bsub < 5:
                continue
            for qt in range(QT):
                vYE.wait(sc_, e_qt(gq, qt))
                S(sc_.activation(dump[:], y_sb[:, qt * DM:(qt + 1) * DM], AF.Square,
                                 accum_out=yss[:, qt:qt + 1]), sST, sst_b(gq, qt))
            vLN.wait(sc_, vln_b(gq))
            S(sc_.activation(ylnv[:], yvar[:], AF.Ln, bias=consts[:, 1:2]))
            S(sc_.activation(yrstd[:], ylnv[:], AF.Exp, scale=-0.5), sRS, srs_b(gq))
        if phases < 3:
            return
        tYMM.wait(sc_, QT * NBG)
        vX2.wait(sc_, VX2_B)
        gX2.wait(sc_, QT * NBG)
        for dc in range(DC):
            for p in range(2):
                gDS[p].wait(sc_, DC + dc + 1)
            sCW2.inc(sc_.dma_start(out=wu_sb[:, dc * HID:(dc + 1) * HID],
                                   in_=wu_scr[dc]), dc + 1)
        def c_stats(gc):
            p4 = (gc % 2) * 4
            b = SST_B + (0 if gc == 0 else 16 * gc - 8)
            sCX2.wait(sc_, gc + 1)
            for i in range(CT):
                S(sc_.activation(dump[:], x2g[gc % 2][:, i * DM:(i + 1) * DM], AF.Copy,
                                 accum_out=csum[:, p4 + i:p4 + i + 1]), sST, b + i + 1)
            for i in range(CT):
                S(sc_.activation(dump[:], x2g[gc % 2][:, i * DM:(i + 1) * DM], AF.Square,
                                 accum_out=csq[:, p4 + i:p4 + i + 1]), sST, b + 4 + i + 1)

        def c_rstd(gc):
            p4 = (gc % 2) * 4
            vLN.wait(sc_, vln_cx(gc))
            S(sc_.activation(clnv[:], cvar[:], AF.Ln, bias=consts[:, 1:2]))
            S(sc_.activation(crstd[:, p4:p4 + 4], clnv[:], AF.Exp, scale=-0.5),
              sRS, srs_cx(gc))

        c_stats(0)
        c_rstd(0)
        for gc in range(NCG):
            if gc == 0:
                sCW2.wait(sc_, DC)
            for hc in range(HC):
                tGUP.wait(sc_, cgup(gc, hc))
                S(sc_.activation(g_sb[hc % 2][:], ps_g[hc % 2][:], AF.Silu),
                  sSIL, csil(gc, hc))
            if gc + 1 < NCG:
                c_stats(gc + 1)
                c_rstd(gc + 1)
            for ct in range(CT):
                vY2A.wait(sc_, cdmm(gc, NHS - 1, ct, 1))
                S(sc_.activation(dump[:], y2acc[:, ct * DM:(ct + 1) * DM], AF.Copy,
                                 accum_out=y2sum[:, ct:ct + 1]), sST, sst_cy(gc, ct) - 1)
                S(sc_.activation(dump[:], y2acc[:, ct * DM:(ct + 1) * DM], AF.Square,
                                 accum_out=y2ss[:, ct:ct + 1]), sST, sst_cy(gc, ct))
            vLN.wait(sc_, vln_cy(gc))
            S(sc_.activation(y2lnv[:], y2var[:], AF.Ln, bias=consts[:, 1:2]))
            S(sc_.activation(y2rstd[:], y2lnv[:], AF.Exp, scale=-0.5), sRS, srs_cy(gc))

    @blk.vector
    def _(ve):
        def V(ins, tok=None, expect=None):
            if tok is None:
                vSELF.n += 1
                ins.then_inc(vSELF.h, 1)
                ve.wait_ge(vSELF.h, vSELF.n)
            else:
                tok.inc(ins, expect)
                ve.wait_ge(tok.h, tok.n)
            return ins

        wdst = [wk_bf, wv_bf, wq_bf, wo_bf]
        for i in range(NW):
            sWL[i % 2].wait(ve, i // 2 + 1)
            dc = i % DC
            V(ve.tensor_copy(wdst[i // DC][:, dc * DM:(dc + 1) * DM],
                             wstage[i % 2][:]), vWC, i + 1)
        sPRE.wait(ve, MISC_N)
        V(ve.tensor_copy(bvb[:], bvc[:]), vWC, NW + 1)
        ts_j = [0]

        def dequant_ts():
            u = ts_j[0]
            if u >= NU:
                return
            ts_j[0] += 1
            sIL[u % 4].wait(ve, u // 4 + 1)
            if u >= 5:
                gDS[(u - 5) % 2].wait(ve, (u - 5) // 2 + 1)
            w = HH if u < 4 * DC else DM
            V(ve.tensor_scalar(bstage[u % 5][:, 0:w], istage[u % 4][:, 0:w],
                               DEQ, None, ALU.mult), gDQ, u + 1)

        def a_statchain(g):
            p4 = (g % 2) * 4
            sST.wait(ve, sst_a(g))
            V(ve.tensor_scalar(amu[:, p4:p4 + 4], asum[:, p4:p4 + 4], INV_DM,
                               None, ALU.mult))
            V(ve.tensor_tensor(alnv2[:], amu[:, p4:p4 + 4], amu[:, p4:p4 + 4],
                               ALU.mult))
            V(ve.scalar_tensor_tensor(avar[:], asq[:, p4:p4 + 4], INV_DM, alnv2[:],
                                      ALU.mult, ALU.subtract), vLN, vln_a(g))

        def a_norms(g):
            p4 = (g % 2) * 4
            sRS.wait(ve, srs_a(g))
            V(ve.scalar_tensor_tensor(anmr[:, p4:p4 + 4], amu[:, p4:p4 + 4], -1.0,
                                      arstd[:, p4:p4 + 4], ALU.mult, ALU.mult))
            if g >= 2 and asub >= 2:
                tTR.wait(ve, ttr_a(g - 2, DC - 1))
            for i in range(4):
                V(ve.tensor_scalar(xln[g % 2][:, i * DM:(i + 1) * DM],
                                   xg[g % 2][:, i * DM:(i + 1) * DM],
                                   arstd[:, p4 + i:p4 + i + 1],
                                   anmr[:, p4 + i:p4 + i + 1], ALU.mult, ALU.add),
                  vNRM, vnrm_a(g, i))

        a_statchain(0)
        a_norms(0)
        for g in range(NAG):
            own = g < NOG
            p4 = (g % 2) * 4
            if asub < 2:
                if g + 1 < NAG:
                    a_statchain(g + 1)
                    a_norms(g + 1)
                continue
            if g >= 2 and asub >= 3:
                tPRJ.wait(ve, prj_base(g - 1))
            for ec in range(DC):
                tTR.wait(ve, ttr_a(g, ec))
                V(ve.tensor_copy(xlnT[g % 2][:, ec * 512:(ec + 1) * 512],
                                 [ps_tr0, a_tr1][ec % 2][:]),
                  vTRC, ttr_a(g, ec))
            if asub < 3:
                if g + 1 < NAG:
                    a_statchain(g + 1)
                    a_norms(g + 1)
                continue
            for i in range(4):
                V4 = 4 * g + i
                gVS[V4 % 4].wait(ve, V4 // 4)
                for h in range(2):
                    vj = 2 * i + h
                    tPRJ.wait(ve, prj_v(g, vj))
                    V(ve.tensor_copy(vstage[i % 4][:, h * 512:(h + 1) * 512],
                                     ps_v[vj % 2][:]), vEV, vev(g, vj))
                if g >= 1:
                    dequant_ts()
            if g >= 1:
                dequant_ts()
            if g + 1 < NAG:
                a_statchain(g + 1)
                a_norms(g + 1)
        while ts_j[0] < NU:
            dequant_ts()
        if phases < 2:
            return
        if bsub >= 5:
            tBO.wait(ve, 2)
            V(ve.tensor_tensor(boP[0:1, 0:512], ps_k[0][0:1, :], bor[0:1, 0:512],
                               ALU.add))
            V(ve.tensor_tensor(boP[0:1, 512:1024], ps_k[1][0:1, :],
                               bor[0:1, 512:1024], ALU.add), vBO, 1)
        def _epi_tail(gq, qt):
            for w in range(2):
                tTR.wait(ve, ttr_b(gq, qt, w))
                V(ve.tensor_copy(onT[:, w * 512:w * 512 + 512],
                                 [ps_tr0, a_tr1][w][:]),
                  vTRC, ttr_b(gq, qt, w))
            if bsub < 5:
                return
            tYMM.wait(ve, e_qt(gq, qt))
            if gq >= 1:
                sST.wait(ve, sst_b(gq - 1, qt))
            if gq == 0 and qt == 0:
                sBOB.wait(ve, 1)
            V(ve.scalar_tensor_tensor(y_sb[:, qt * DM:(qt + 1) * DM],
                                      ps_o[qt % 2][:], linv[:, qt:qt + 1],
                                      boB[:], ALU.mult, ALU.add,
                                      accum_out=ysum[:, qt:qt + 1]),
              vYE, e_qt(gq, qt))

        def x2_pair(gq_, qt):
            t = QT * gq_ + qt
            gX2.wait(ve, t)
            sXO.wait(ve, t + 1)
            V(ve.tensor_scalar(eTMP[:], y_sb[:, qt * DM:(qt + 1) * DM],
                               yrstd[:, qt:qt + 1],
                               ynmr[:, qt:qt + 1], ALU.mult, ALU.add))
            V(ve.tensor_tensor(x2st[:], eTMP[:], xo_sb[:], ALU.add),
              vX2, e_qt(gq_, qt))

        for gq in range(NBG):
            for slot in range(NSC + 1):
                for j in range(KCS):
                    if slot < NSC:
                        sc = slot
                        kc = j
                        kcg = kcg_of(gq, sc, kc)
                        hh = (kcg % 2) * BQ
                        sTH.wait(ve, 2 * kcg + 2)
                        V(ve.scalar_tensor_tensor(ta[:, hh:hh + BQ], ps_s[:], 0.0,
                                                  ta[:, hh:hh + BQ], ALU.max, ALU.add),
                          vTH, 2 * kcg + 1)
                        V(ve.scalar_tensor_tensor(tb[:, hh:hh + BQ], tb[:, hh:hh + BQ],
                                                  1.5, ta[:, hh:hh + BQ], ALU.mult,
                                                  ALU.add), vTH, 2 * kcg + 2)
                    if slot > 0 and bsub >= 2:
                        sc = slot - 1
                        scg = gq * NSC + sc
                        qt = j
                        tPV.wait(ve, tpv(scg, qt))
                        dst = o_acc[:, qt * DM:(qt + 1) * DM]
                        if sc == 0:
                            ins = ve.tensor_scalar(dst, ps_o[qt % 2][:], 1.0, None,
                                                   ALU.mult)
                        else:
                            ins = ve.tensor_tensor(dst, dst, ps_o[qt % 2][:], ALU.add)
                        V(ins, vOE, QT * scg + qt + 1)
                        if slot == NSC and bsub >= 3:
                            ins = ve.tensor_tensor(l_acc[:, qt:qt + 1],
                                                   l_acc[:, qt:qt + 1],
                                                   ps_l[:, qt:qt + 1], ALU.add)
                            if qt == QT - 1:
                                V(ins, vLA, scg + 1)
                            else:
                                V(ins)
                            V(ve.reciprocal(linv[:, qt:qt + 1], l_acc[:, qt:qt + 1]))
                            if qt >= 1 and bsub >= 4:
                                _epi_tail(gq, qt - 1)
                            if QT * gq + qt > 0 and bsub >= 4:
                                tTR.wait(ve, ttr_b(gq, qt, 0) - 1)
                            V(ve.tensor_scalar(onb[:], dst, 1.0, None, ALU.mult),
                              vON, e_qt(gq, qt))
                if slot > 0 and bsub >= 2 and slot < NSC:
                    sc = slot - 1
                    scg = gq * NSC + sc
                    if sc == 0:
                        ins = ve.tensor_scalar(l_acc[:], ps_l[:], 1.0, None, ALU.mult)
                    else:
                        ins = ve.tensor_tensor(l_acc[:], l_acc[:], ps_l[:], ALU.add)
                    V(ins, vLA, scg + 1)
                if bsub >= 6 and gq >= 1 and slot < NSC:
                    for qtx in range(slot * QT // NSC, (slot + 1) * QT // NSC):
                        x2_pair(gq - 1, qtx)
            if bsub < 4:
                continue
            _epi_tail(gq, QT - 1)
            if bsub < 5:
                continue
            sST.wait(ve, sst_b(gq, QT - 1))
            V(ve.tensor_scalar(ymu[:], ysum[:], INV_DM, None, ALU.mult))
            V(ve.tensor_tensor(ylnv2[:], ymu[:], ymu[:], ALU.mult))
            V(ve.scalar_tensor_tensor(yvar[:], yss[:], INV_DM, ylnv2[:], ALU.mult,
                                      ALU.subtract), vLN, vln_b(gq))
            sRS.wait(ve, srs_b(gq))
            V(ve.scalar_tensor_tensor(ynmr[:], ymu[:], -1.0, yrstd[:], ALU.mult,
                                      ALU.mult))
            if bsub < 6:
                continue
            if gq == NBG - 1:
                for qt in range(QT):
                    x2_pair(gq, qt)
        if phases < 3:
            return
        def c_chain(gc):
            p4 = (gc % 2) * 4
            sST.wait(ve, sst_cx(gc))
            V(ve.tensor_scalar(cmu[:, p4:p4 + 4], csum[:, p4:p4 + 4], INV_DM,
                               None, ALU.mult))
            V(ve.tensor_tensor(clnv2[:], cmu[:, p4:p4 + 4], cmu[:, p4:p4 + 4],
                               ALU.mult))
            V(ve.scalar_tensor_tensor(cvar[:], csq[:, p4:p4 + 4], INV_DM, clnv2[:],
                                      ALU.mult, ALU.subtract), vLN, vln_cx(gc))

        def c_norms(gc):
            p4 = (gc % 2) * 4
            sRS.wait(ve, srs_cx(gc))
            V(ve.scalar_tensor_tensor(cnmr[:, p4:p4 + 4], cmu[:, p4:p4 + 4], -1.0,
                                      crstd[:, p4:p4 + 4], ALU.mult, ALU.mult))
            if gc >= 1:
                tTR.wait(ve, ttr_c(gc - 1, DC - 1))
            for i in range(CT):
                V(ve.tensor_scalar(x2ln[:, i * DM:(i + 1) * DM],
                                   x2g[gc % 2][:, i * DM:(i + 1) * DM],
                                   crstd[:, p4 + i:p4 + i + 1],
                                   cnmr[:, p4 + i:p4 + i + 1], ALU.mult, ALU.add),
                  vNRM, vnrm_c(gc, i))

        def c_copies(gc):
            if gc >= 1:
                tGUP.wait(ve, cgup(gc - 1, HC - 1))
            for ec in range(DC):
                tTR.wait(ve, ttr_c(gc, ec))
                V(ve.tensor_copy(x2lnT[:, ec * 512:(ec + 1) * 512],
                                 [ps_tr0, c_tr1][ec % 2][:]),
                  vTRC, ttr_c(gc, ec))

        c_chain(0)
        c_norms(0)
        c_copies(0)
        for gc in range(NCG):
            for hcs in range(NHS):
                for hc in range(hcs * HSUP, (hcs + 1) * HSUP):
                    ghs = gc * NHS + hc // HSUP
                    if ghs >= 2:
                        tDMM.wait(ve, (ghs - 1) * CT * 2)
                    sSIL.wait(ve, csil(gc, hc))
                    hsl = (hc % (2 * HSUP)) * 512
                    V(ve.tensor_tensor(hT[:, hsl:hsl + 512], ps_u[hc % 2][:],
                                       g_sb[hc % 2][:], ALU.mult), vHT, cht(gc, hc))
                if hcs == NHS - 1 and gc + 1 < NCG:
                    c_chain(gc + 1)
                    c_norms(gc + 1)
                for qt in range(CT):
                    for half in range(2):
                        tDMM.wait(ve, cdmm(gc, hcs, qt, half))
                        dst = y2acc[:, qt * DM + half * 512:qt * DM + half * 512 + 512]
                        if hcs == 0:
                            ins = ve.tensor_scalar(dst, ps_y2[:, half * 512:half * 512 + 512],
                                                   1.0, None, ALU.mult)
                        else:
                            ins = ve.tensor_tensor(dst, dst,
                                                   ps_y2[:, half * 512:half * 512 + 512],
                                                   ALU.add)
                        V(ins, vY2A, cdmm(gc, hcs, qt, half))
            if gc + 1 < NCG:
                c_copies(gc + 1)
            sST.wait(ve, sst_cy(gc, CT - 1))
            V(ve.tensor_scalar(y2mu[:], y2sum[:], INV_DM, None, ALU.mult))
            V(ve.tensor_tensor(y2lnv2[:], y2mu[:], y2mu[:], ALU.mult))
            V(ve.scalar_tensor_tensor(y2var[:], y2ss[:], INV_DM, y2lnv2[:],
                                      ALU.mult, ALU.subtract), vLN, vln_cy(gc))
            sRS.wait(ve, srs_cy(gc))
            V(ve.scalar_tensor_tensor(y2nmr[:], y2mu[:], -1.0, y2rstd[:], ALU.mult,
                                      ALU.mult))
            for ct in range(CT):
                t_o = CT * gc + ct
                gOS[t_o % 2].wait(ve, t_o // 2)
                V(ve.tensor_scalar(eTMP[:], y2acc[:, ct * DM:(ct + 1) * DM],
                                   y2rstd[:, ct:ct + 1], y2nmr[:, ct:ct + 1],
                                   ALU.mult, ALU.add))
                V(ve.tensor_tensor(outst[ct % 2][:], eTMP[:],
                                   x2g[gc % 2][:, ct * DM:(ct + 1) * DM], ALU.add),
                  vX2, vx2_c(gc, ct))

    @blk.tensor
    def _(te):
        sPRE.wait(te, MISC_N)
        for g in range(NAG):
            own = g < NOG
            if asub < 2:
                continue
            vNRM.wait(te, vnrm_a(g, 3))
            for ec in range(DC):
                if DC * g + ec >= 2:
                    vTRC.wait(te, DC * g + ec - 1)
                trb = [ps_tr0, a_tr1][ec % 2]
                last = None
                for i in range(4):
                    last = te.transpose(
                        trb[:, i * 128:i * 128 + 128],
                        xln[g % 2][:, i * DM + ec * 128:i * DM + ec * 128 + 128],
                        ident[:])
                tTR.inc(last, ttr_a(g, ec))
            if asub < 3:
                continue
            vTRC.wait(te, ttr_a(g, DC - 1))
            if g == 0:
                vWC.wait(te, DC)
            for ec in range(DC):
                K = DC * g + ec
                if K >= 2:
                    sKQ.wait(te, kq_k_glob(K - 2))
                last = None
                for dc in range(DC):
                    last = te.matmul(ps_k[ec % 2][:],
                                     wk_bf[:, dc * DM + ec * 128:dc * DM + ec * 128 + 128],
                                     xlnT[g % 2][:, dc * 512:(dc + 1) * 512],
                                     start=(dc == 0), stop=(dc == DC - 1))
                tPRJ.inc(last, prj_k(g, ec))
            if g == 0:
                vWC.wait(te, 2 * DC)
            for vj in range(8):
                i, h = vj // 2, vj % 2
                if 8 * g + vj >= 2:
                    vEV.wait(te, 8 * g + vj - 1)
                last = None
                for dc in range(DC):
                    last = te.matmul(ps_v[vj % 2][:],
                                     xlnT[g % 2][:, dc * 512 + i * 128:dc * 512 + i * 128 + 128],
                                     wv_bf[:, dc * DM + h * 512:dc * DM + h * 512 + 512],
                                     start=(dc == 0), stop=(dc == DC - 1))
                tPRJ.inc(last, prj_v(g, vj))
            if own:
                if g == 0:
                    vWC.wait(te, 3 * DC)
                for ec in range(DC):
                    K = DC * g + ec
                    if K >= 2:
                        sKQ.wait(te, kq_q_glob(K - 2))
                    last = None
                    for dc in range(DC):
                        last = te.matmul(ps_q[ec % 2][:],
                                         wq_bf[:, dc * DM + ec * 128:dc * DM + ec * 128 + 128],
                                         xlnT[g % 2][:, dc * 512:(dc + 1) * 512],
                                         start=(dc == 0), stop=(dc == DC - 1))
                    tPRJ.inc(last, prj_q(g, ec))
        if phases < 2:
            return
        if bsub >= 5:
            sKQ.wait(te, KQ_END)
            vWC.wait(te, NW + 1)
        if bsub >= 5:
            for half in range(2):
                last = None
                for dc in range(DC):
                    last = te.matmul(ps_k[half][0:1, :], bvb[:, dc:dc + 1],
                                     wo_bf[:, dc * DM + half * 512:dc * DM + half * 512 + 512],
                                     start=(dc == 0), stop=(dc == DC - 1))
                tBO.inc(last, half + 1)
        def _epi_pe(gq, qt):
            vON.wait(te, e_qt(gq, qt))
            if qt == 0:
                kl = KCS * NSC * (gq + 1)
                sTH.wait(te, 2 * kl)
                vTH.wait(te, 2 * kl - 1)
            for w in range(2):
                vTRC.wait(te, ttr_b(gq, qt, w) - 2)
                trb = [ps_tr0, a_tr1][w]
                last = None
                for dc in range(4 * w, 4 * w + 4):
                    last = te.transpose(trb[:, (dc % 4) * 128:(dc % 4) * 128 + 128],
                                        onb[:, dc * 128:(dc + 1) * 128], ident[:])
                tTR.inc(last, ttr_b(gq, qt, w))
            if bsub < 5:
                return
            vTRC.wait(te, ttr_b(gq, qt, 1))
            if qt >= 2:
                vYE.wait(te, e_qt(gq, qt) - 2)
            last = None
            for h in range(2):
                for dc in range(DC):
                    last = te.matmul(
                        ps_o[qt % 2][:, h * 512:h * 512 + 512],
                        onT[:, dc * 128:(dc + 1) * 128],
                        wo_bf[:, dc * DM + h * 512:dc * DM + h * 512 + 512],
                        start=(dc == 0), stop=(dc == DC - 1))
            tYMM.inc(last, e_qt(gq, qt))

        for gq in range(NBG):
            for slot in range(NSC + 1):
                for j in range(KCS):
                    if slot < NSC:
                        sc = slot
                        scg = gq * NSC + sc
                        kc = j
                        kcg = kcg_of(gq, sc, kc)
                        if kcg == 0 and bsub >= 5:
                            vBO.wait(te, 1)
                        if kc == 0 and sc == 0 and gq > 0 and bsub >= 4:
                            vTRC.wait(te, ttr_b(gq - 1, QT - 1, 1))
                        hsc = 2 * scg + (kc // 4)
                        if kc % 4 == 0:
                            sKT[kc // 4].wait(te, DC * (scg - gq + 1))
                        if kc == 0 and sc == 0:
                            sQT.wait(te, DC * gq + DC)
                        if kcg > 0:
                            sTH.wait(te, 2 * kcg)
                            vTH.wait(te, 2 * kcg - 1)
                        last = None
                        for ec in range(DC):
                            for h2 in range(2):
                                kof = ec * 512 + (kc % 4) * 128
                                last = te.matmul(
                                    ps_s[:, h2 * 512:h2 * 512 + 512],
                                    kT_sb[hsc % 2][:, kof:kof + 128],
                                    qT_sb[:, ec * BQ + h2 * 512:ec * BQ + h2 * 512 + 512],
                                    start=(ec == 0), stop=(ec == DC - 1))
                        tSMM.inc(last, kcg + 1)
                    if slot > 0 and bsub >= 2:
                        sc = slot - 1
                        scg = gq * NSC + sc
                        qt = j
                        psl = (scg % 2) * KCS * BQ
                        for kc in range(KCS):
                            kcg = kcg_of(gq, sc, kc)
                            if qt == 0:
                                sPT.wait(te, kcg + 1)
                            lv = scg - gq
                            if kc == 0:
                                if qt == 0:
                                    sV[lv % 2].wait(te, lv // 2 + 1)
                                    vLA.wait(te, scg)
                                vOE.wait(te, QT * scg + qt - 1)
                                if slot == NSC and qt >= 2 and bsub >= 5:
                                    vYE.wait(te, e_qt(gq, qt) - 2)
                            pts = PT[:, psl + kc * BQ + qt * 128:psl + kc * BQ + qt * 128 + 128]
                            te.matmul(ps_o[qt % 2][:, 0:512], pts,
                                      v_sb[lv % 2][:, kc * DM:kc * DM + 512],
                                      start=(kc == 0), stop=(kc == KCS - 1),
                                      skip_group_check=True)
                            te.matmul(ps_o[qt % 2][:, 512:1024], pts,
                                      v_sb[lv % 2][:, kc * DM + 512:kc * DM + 1024],
                                      start=(kc == 0), stop=(kc == KCS - 1),
                                      skip_group_check=True)
                            last = te.matmul(ps_l[:, qt:qt + 1], pts, ones[:, 0:1],
                                             start=(kc == 0), stop=(kc == KCS - 1),
                                             skip_group_check=True)
                            if kc == KCS - 1:
                                tPV.inc(last, tpv(scg, qt))
                        if slot == NSC and qt >= 1 and bsub >= 4:
                            _epi_pe(gq, qt - 1)
            if bsub < 4:
                continue
            _epi_pe(gq, QT - 1)
        if phases < 3:
            return
        for gc in range(NCG):
            vNRM.wait(te, vnrm_c(gc, CT - 1))
            for ec in range(DC):
                if TTR_B + DC * gc + ec >= TTR_B + 2:
                    vTRC.wait(te, ttr_c(gc, ec) - 2)
                trb = [ps_tr0, c_tr1][ec % 2]
                last = None
                for i in range(CT):
                    last = te.transpose(
                        trb[:, i * 128:i * 128 + 128],
                        x2ln[:, i * DM + ec * 128:i * DM + ec * 128 + 128], ident[:])
                tTR.inc(last, ttr_c(gc, ec))
            vTRC.wait(te, ttr_c(gc, DC - 1))
            if gc == 0:
                sCW.wait(te, DC)
            for hcs in range(NHS):
                for hc in range(hcs * HSUP, (hcs + 1) * HSUP):
                    ghc = HC * gc + hc
                    if ghc >= 2:
                        sSIL.wait(te, ghc - 1)
                    last = None
                    for dc in range(DC):
                        last = te.matmul(ps_g[hc % 2][:],
                                         wg_sb[:, dc * HID + hc * 128:dc * HID + hc * 128 + 128],
                                         x2lnT[:, dc * 512:(dc + 1) * 512],
                                         start=(dc == 0), stop=(dc == DC - 1))
                    if ghc == 0:
                        sCW2.wait(te, DC)
                    if ghc >= 2:
                        vHT.wait(te, ghc - 1)
                    for dc in range(DC):
                        last = te.matmul(ps_u[hc % 2][:],
                                         wu_sb[:, dc * HID + hc * 128:dc * HID + hc * 128 + 128],
                                         x2lnT[:, dc * 512:(dc + 1) * 512],
                                         start=(dc == 0), stop=(dc == DC - 1))
                    tGUP.inc(last, cgup(gc, hc))
                ghcs_ = gc * NHS + hcs
                sWD[ghcs_ % 2].wait(te, ghcs_ // 2 + 1)
                vHT.wait(te, HC * gc + hcs * HSUP + HSUP)
                for qt in range(CT):
                    for half in range(2):
                        H = cdmm(gc, hcs, qt, half) - 1
                        if H >= 2:
                            vY2A.wait(te, H - 1)
                        last = None
                        for j in range(HSUP):
                            hc = hcs * HSUP + j
                            hsl = (hc % (2 * HSUP)) * 512
                            last = te.matmul(
                                ps_y2[:, half * 512:half * 512 + 512],
                                hT[:, hsl + qt * 128:hsl + qt * 128 + 128],
                                wdsup[hcs % 2][:, j * DM + half * 512:j * DM + half * 512 + 512],
                                start=(j == 0), stop=(j == HSUP - 1))
                        tDMM.inc(last, cdmm(gc, hcs, qt, half))

    nc._cfg = dict(SEQ=SEQ, DM=DM, HID=HID, OWN=OWN)
    return nc


def _prep_inputs(L0_attn_x, L0_attn_Wq, L0_attn_bq, L0_attn_Wk, L0_attn_bk,
                 L0_attn_Wv, L0_attn_bv, L0_attn_Wo, L0_attn_bo,
                 L0_ffn_w_gate, L0_ffn_w_up, L0_ffn_w_down):
    DM = L0_attn_Wq.shape[0]
    DC = DM // 128
    x = np.ascontiguousarray(np.asarray(L0_attn_x, dtype=np.float32))
    shared = {
        "wk_t": np.ascontiguousarray(np.asarray(L0_attn_Wk, np.float32).T),
        "wv_t": np.ascontiguousarray(np.asarray(L0_attn_Wv, np.float32).T),
        "wq_t": np.ascontiguousarray(np.asarray(L0_attn_Wq, np.float32).T),
        "wo_t": np.ascontiguousarray(np.asarray(L0_attn_Wo, np.float32).T),
        "bq_c": np.ascontiguousarray(np.asarray(L0_attn_bq, np.float32).reshape(DC, 128).T),
        "bk_c": np.ascontiguousarray(np.asarray(L0_attn_bk, np.float32).reshape(DC, 128).T),
        "bv_c": np.ascontiguousarray(np.asarray(L0_attn_bv, np.float32).reshape(DC, 128).T),
        "bo_r": np.ascontiguousarray(np.asarray(L0_attn_bo, np.float32).reshape(1, DM)),
        "wg_i": np.ascontiguousarray(np.asarray(L0_ffn_w_gate, np.int32).T),
        "wu_i": np.ascontiguousarray(np.asarray(L0_ffn_w_up, np.int32).T),
        "wd_i": np.ascontiguousarray(np.asarray(L0_ffn_w_down, np.int32).T),
        "ident_h": np.eye(128, dtype=BFNP),
        "ones_h": np.ones((128, 512), dtype=BFNP),
        "consts_h": np.stack([np.full(128, 0.6 - C_OFF, np.float32),
                              np.full(128, EPS, np.float32),
                              np.zeros(128, np.float32),
                              np.zeros(128, np.float32)], axis=1),
    }
    return x, shared


def kernel(L0_attn_x, L0_attn_Wq, L0_attn_bq, L0_attn_Wk, L0_attn_bk, L0_attn_Wv,
           L0_attn_bv, L0_attn_Wo, L0_attn_bo, L0_ffn_w_gate, L0_ffn_w_up,
           L0_ffn_w_down):
    x, shared = _prep_inputs(L0_attn_x, L0_attn_Wq, L0_attn_bq, L0_attn_Wk,
                             L0_attn_bk, L0_attn_Wv, L0_attn_bv, L0_attn_Wo,
                             L0_attn_bo, L0_ffn_w_gate, L0_ffn_w_up, L0_ffn_w_down)
    B, S, DM = x.shape
    OWN = S // 2
    nc = build_program(SEQ=S, DM=DM, HID=L0_ffn_w_gate.shape[0])
    in_maps = []
    for core in range(8):
        b, h = core // 2, core % 2
        m = dict(shared)
        m["x_own"] = np.ascontiguousarray(x[b, h * OWN:(h + 1) * OWN])
        m["x_oth"] = np.ascontiguousarray(x[b, (1 - h) * OWN:(2 - h) * OWN])
        in_maps.append(m)
    res = run_bass_kernel_spmd(nc, in_maps, core_ids=list(range(8)))
    out = np.empty((B, S, DM), dtype=np.float32)
    for core in range(8):
        b, h = core // 2, core % 2
        out[b, h * OWN:(h + 1) * OWN] = res.results[core]["y_out"]
    return out
